# version 28 (final)
# baseline (speedup 1.0000x reference)
import numpy as np
import concourse.bass as bass
import concourse.mybir as mybir
from concourse.bass_utils import run_bass_kernel_spmd

F32 = mybir.dt.float32
F32R = mybir.dt.float32r
BF16 = mybir.dt.bfloat16
AF = mybir.ActivationFunctionType
ALU = mybir.AluOpType
AX = mybir.AxisListType

NCORE = 8
BPC = 2
SEQ = 2048
DM = 1024
DI = 2048
DH = 4096
DS = 16
C = 256
W = C + 2
NT = C // 128
ND = DM // 128
NCH = DI // 128
NH = DH // 128
SEQCH = SEQ // C
CHUNKS = BPC * SEQCH

T_WIN = 2 * NCH * ND
T_WOUT = ND * NCH
T_S1 = NH * ND
T_S2 = ND * NH
TILES = T_WIN + T_WOUT + T_S1 + T_S2
SLOT_T = 16
SLOTS_PER_CHUNK = TILES // SLOT_T
RING_SLOTS = 11
EPS = 1e-5


class _Ev:
    __slots__ = ("sem", "val")

    def __init__(self, sem, val):
        self.sem = sem
        self.val = val


def _combine(*evs):
    out = []
    for e in evs:
        if e is None:
            continue
        if isinstance(e, list):
            out.extend(e)
        else:
            out.append(e)
    return out


class _Eng:

    NPOOL = 12

    def __init__(self, prog, semname, dma=False):
        self.prog = prog
        self.semname = semname
        self.dma = dma
        self.ops = []
        self.count = 0
        self.ndma = 0
        self.waited = {}

    def semnames(self):
        if not self.dma:
            return [self.semname]
        return [f"{self.semname}{j}" for j in range(self.NPOOL)]

    def wait(self, ev):
        if ev is None:
            return
        if isinstance(ev, list):
            for e in ev:
                self.wait(e)
            return
        if ev.sem == self.semname and not self.dma:
            return
        if self.waited.get(ev.sem, -1) >= ev.val:
            return
        self.waited[ev.sem] = ev.val
        sem, val = ev.sem, ev.val
        self.ops.append(lambda e, sem=sem, val=val: e.wait_ge(self.prog.sems[sem], val))

    def wait_own(self, ev):
        if ev is None:
            return
        if isinstance(ev, list):
            for e in ev:
                self.wait_own(e)
            return
        if self.waited.get(ev.sem, -1) >= ev.val:
            return
        self.waited[ev.sem] = ev.val
        sem, val = ev.sem, ev.val
        self.ops.append(lambda e, sem=sem, val=val: e.wait_ge(self.prog.sems[sem], val))

    def emitf(self, fn):
        ev = self.emit(fn, signal=True)
        self.waited.pop(ev.sem, None)
        self.wait_own(ev)
        return ev

    def emit(self, fn, signal=False):
        if self.dma:
            i = self.ndma
            self.ndma += 1
            j = i % self.NPOOL
            k = i // self.NPOOL
            sem = f"{self.semname}{j}"
            if k >= 1:
                self.wait(_Ev(sem, 16 * k))
            ev = _Ev(sem, 16 * (k + 1))

            def wrapped(e, fn=fn, sem=sem):
                fn(e).then_inc(self.prog.sems[sem], 16)
            self.ops.append(wrapped)
            return ev
        ev = None
        if signal:
            self.count += 1
            ev = _Ev(self.semname, self.count)

            def wrapped(e, fn=fn):
                fn(e).then_inc(self.prog.sems[self.semname], 1)
            self.ops.append(wrapped)
        else:
            self.ops.append(fn)
        return ev


class _Prog:
    def __init__(self):
        self.sems = {}


def _mk_fence(eng, fence_buf):
    def fence():
        eng.emitf(lambda e: e.tensor_copy(fence_buf[:, 0:4], fence_buf[:, 4:8]))
        return eng.emitf(lambda e: e.tensor_copy(fence_buf[:, 4:8], fence_buf[:, 0:4]))
    return fence


def build_program(debug=False, n_chunks=CHUNKS):
    nc = bass.Bass("TRN2", target_bir_lowering=False, detect_race_conditions=False)

    x_d = nc.dram_tensor("x", [BPC, SEQ, DM], F32, kind="ExternalInput").ap()
    win_d = nc.dram_tensor("W_in", [DM, 2 * DI], F32, kind="ExternalInput").ap()
    cw_d = nc.dram_tensor("conv_w", [DI, 3], F32, kind="ExternalInput").ap()
    cb_d = nc.dram_tensor("conv_b", [DI], F32, kind="ExternalInput").ap()
    wb_d = nc.dram_tensor("W_B", [DI, DS], F32, kind="ExternalInput").ap()
    wc_d = nc.dram_tensor("W_C", [DI, DS], F32, kind="ExternalInput").ap()
    a_d = nc.dram_tensor("A", [DS], F32, kind="ExternalInput").ap()
    dv_d = nc.dram_tensor("Dv", [DI], F32, kind="ExternalInput").ap()
    wo_d = nc.dram_tensor("W_out", [DI, DM], F32, kind="ExternalInput").ap()
    w1_d = nc.dram_tensor("w1", [DH, DM], F32, kind="ExternalInput").ap()
    b1_d = nc.dram_tensor("b1", [DH], F32, kind="ExternalInput").ap()
    w2_d = nc.dram_tensor("w2", [DM, DH], F32, kind="ExternalInput").ap()
    b2_d = nc.dram_tensor("b2", [DM], F32, kind="ExternalInput").ap()
    eye_d = nc.dram_tensor("eye", [128, 128], F32, kind="ExternalInput").ap()
    cz_d = nc.dram_tensor("czeros", [128, 256], F32, kind="ExternalInput").ap()
    co_d = nc.dram_tensor("cones", [128, 256], F32, kind="ExternalInput").ap()
    cm_d = nc.dram_tensor("cmagic", [128, 8], mybir.dt.int32, kind="ExternalInput").ap()
    out_d = nc.dram_tensor("out", [BPC, SEQ, DM], F32, kind="ExternalOutput").ap()
    ws_d = nc.dram_tensor("wstream", [SLOTS_PER_CHUNK, 128, SLOT_T * 128], BF16).ap()

    A_ = nc.alloc_sbuf_tensor
    ring = A_("ring", [128, RING_SLOTS * SLOT_T, 128], BF16).ap()
    x_tm = A_("x_tm", [128, 2, NT, DM], F32).ap()
    xrow_f = A_("xrow_f", [1, 2, DM], F32).ap()

    x_fm = A_("x_fm", [128, 2, ND, W], BF16).ap()
    xc = A_("xc", [128, NCH, C], F32R).ap()
    xc_f = xc.bitcast(F32)
    cv1 = A_("cv1", [128, 4, C], F32).ap()
    cv2 = A_("cv2", [128, 4, C], F32).ap()
    sr = A_("sr", [128, 4, C], F32).ap()

    y_bf = A_("y_bf", [128, NCH, C], BF16).ap()
    t2 = cv2
    s_sc = A_("s_sc", [16, C], F32).ap()
    carry = A_("carry", [16, 1], F32).ap()
    sc_r = A_("sc_r", [128, C], F32R).ap()
    onesp = A_("onesp", [128, 128], F32R).ap()
    zeroC = A_("zeroC", [128, C], F32).ap()
    attn_sb = A_("attn_sb", [128, ND, C], F32R).ap()
    attn_f = attn_sb.bitcast(F32)
    r_tm = A_("r_tm", [128, NT, DM], F32).ap()
    x1_tm = A_("x1_tm", [128, NT, DM], F32R).ap()
    x1_f = x1_tm.bitcast(F32)
    x1_fm = A_("x1_fm", [128, ND, C], BF16).ap()
    h_bf = A_("h_bf", [128, NH, C], BF16).ap()
    out_tm = A_("out_tm", [128, 2, NT, DM], F32).ap()
    junk = A_("junk", [128, 512], F32).ap()
    eye_f = A_("eye_f", [128, 128], F32).ap()
    eye_bf = A_("eye_bf", [128, 128], BF16).ap()
    eye_r = A_("eye_r", [128, 128], F32R).ap()
    wtap = A_("wtap", [128, NCH, 3], F32).ap()
    cb_sb = A_("cb_sb", [128, NCH], F32).ap()
    dv_sb = A_("dv_sb", [128, NCH], F32).ap()
    wb_f = A_("wb_f", [128, NCH, DS], F32).ap()
    wc_f = A_("wc_f", [128, NCH, DS], F32).ap()
    wb_r = A_("wb_r", [128, NCH, DS], F32R).ap()
    wc_r = A_("wc_r", [128, NCH, DS], F32R).ap()
    a_sb = A_("a_sb", [16, 1], F32).ap()
    dec = A_("dec", [16, 1], F32).ap()
    dec_bc = A_("dec_bc", [16, C], F32).ap()
    ones_sb = A_("ones_sb", [128, 256], F32).ap()
    ones_sc = ones_sb[0:16, :]
    zbf = A_("zbf", [128, 8], BF16).ap()
    magic_sb = A_("magic_sb", [128, 8], mybir.dt.int32).ap()
    tmp16 = A_("tmp16", [16, 1], F32).ap()
    tmp16b = A_("tmp16b", [16, 1], F32).ap()
    b1_sb = A_("b1_sb", [128, NH], F32).ap()
    al1 = A_("al1", [128, NH], F32).ap()
    b2_sb = A_("b2_sb", [128, ND], F32).ap()
    al2 = A_("al2", [128, ND], F32).ap()
    st_sum = A_("st_sum", [128, 2, NT], F32).ap()
    st_sq = A_("st_sq", [128, 2, NT], F32).ap()
    st_m = A_("st_m", [128, NT], F32).ap()
    st_var = A_("st_var", [128, NT], F32).ap()
    st_inv = A_("st_inv", [128, NT], F32).ap()
    st_nmi = A_("st_nmi", [128, NT], F32).ap()
    st_t = A_("st_t", [128, NT], F32).ap()
    st_t2 = A_("st_t2", [128, NT], F32).ap()
    xt_bf = A_("xt_bf", [128, NT, 1024], BF16).ap()
    stg2 = A_("stg2", [128, 2, SLOT_T * 128], BF16).ap()
    stg_state = {"n": 0, "busy": [None, None], "cur": None}

    def stg_next():
        b = stg_state["n"] % 2
        stg_state["n"] += 1
        return stg2[:, b, :], b

    def stg_stored(b, ev):
        stg_state["busy"][b] = ev
    srow_bf = A_("srow_bf", [128, DH], BF16).ap()
    kr_bf = h_bf.rearrange("p a b -> p (a b)")
    out_flat = out_tm.rearrange("p a b c -> p (a b c)")
    kr2_bf = out_flat.bitcast(BF16)
    wrow_f = out_flat
    krow_aps = [
        kr_bf[:, 0:4096],
        kr_bf[:, 4096:8192],
        kr2_bf[:, 0:4096],
        kr2_bf[:, 4096:8192],
        srow_bf,
        r_tm.rearrange("p a b -> p (a b)").bitcast(BF16),
        ring[:, 8 * SLOT_T:10 * SLOT_T, :].rearrange("p a b -> p (a b)"),
        y_bf.rearrange("p a b -> p (a b)"),
    ]

    PB = [nc.alloc_psum_tensor(f"pb{i}", [128, 512], F32).ap() for i in range(8)]
    bc_ps = PB[4]
    ys_ps = PB[5][:, 0:C]

    fence_buf = A_("fence_buf", [128, 8], F32).ap()

    p = _Prog()
    T = _Eng(p, "s_pe")
    V = _Eng(p, "s_v")
    S = _Eng(p, "s_s")
    Q = _Eng(p, "s_ring", dma=True)
    G = _Eng(p, "s_g", dma=True)

    GC = _Eng(p, "s_gc")
    GC.ops = G.ops

    vfence = _mk_fence(V, fence_buf)

    def sfence():
        S.emitf(lambda e: e.copy(fence_buf[:, 0:4], fence_buf[:, 4:8]))
        return S.emitf(lambda e: e.copy(fence_buf[:, 4:8], fence_buf[:, 0:4]))

    dbg_outs = {}
    if debug == "ws":
        wsdump = nc.dram_tensor("wsdump", [SLOTS_PER_CHUNK, 128, SLOT_T * 128], BF16,
                                kind="ExternalOutput").ap()
    if debug and debug != "ws":
        for nm, shp, dt_ in [("d_xc", [128, NCH, C], F32R), ("d_ys", [128, C], F32),
                             ("d_x1", [128, NT, DM], F32), ("d_h", [128, NH, C], BF16),
                             ("d_y", [128, NCH, C], BF16), ("d_ffn", [128, ND, C], F32),
                             ("d_r", [128, NT, DM], F32)]:
            dbg_outs[nm] = nc.dram_tensor(nm, shp, dt_, kind="ExternalOutput").ap()
        for nm, shp, dt_ in [("d_ssc", [16, C], F32), ("d_scr", [128, C], F32R),
                             ("d_bc", [16, 2 * C], F32), ("d_dec", [16, C], F32),
                             ("d_ones", [128, 128], F32R), ("d_ys2", [128, NT, DM], F32),
                             ("d_snap", [128, 2, NT], F32), ("d_snap2", [128, 2, NT], F32),
                             ("d_snap3", [128, 4 * NT], F32),
                             ("d_al1", [128, NH], F32), ("d_b1", [128, NH], F32),
                             ("d_al2", [128, ND], F32), ("d_x1fm", [128, ND, C], BF16),
                             ("d_stm", [128, NT], F32), ("d_stv", [128, NT], F32),
                             ("d_sti", [128, NT], F32), ("d_ssum", [128, 2, NT], F32),
                             ("d_ssq", [128, 2, NT], F32)]:
            dbg_outs[nm] = nc.dram_tensor(nm, shp, dt_, kind="ExternalOutput").ap()

    def _slow(g, dst, srcap):
        return g.dma_start(dst, srcap, allow_slow_non_contiguous=True)

    _ce = []
    e_eye = G.emit(lambda g: g.dma_start(eye_f, eye_d), signal=True)
    _ce.append(e_eye)
    _ce.append(G.emit(lambda g: _slow(g, wtap, cw_d.rearrange("(j p) t -> p j t", p=128))))
    _ce.append(G.emit(lambda g: _slow(g, cb_sb, cb_d.rearrange("(j p) -> p j", p=128))))
    _ce.append(G.emit(lambda g: _slow(g, dv_sb, dv_d.rearrange("(j p) -> p j", p=128))))
    _ce.append(G.emit(lambda g: _slow(g, wb_f, wb_d.rearrange("(j p) s -> p j s", p=128))))
    _ce.append(G.emit(lambda g: _slow(g, wc_f, wc_d.rearrange("(j p) s -> p j s", p=128))))
    _ce.append(G.emit(lambda g: _slow(g, a_sb, a_d.rearrange("(s o) -> s o", o=1))))
    _ce.append(G.emit(lambda g: _slow(g, b1_sb, b1_d.rearrange("(j p) -> p j", p=128))))
    _ce.append(G.emit(lambda g: _slow(g, b2_sb, b2_d.rearrange("(j p) -> p j", p=128))))
    e_const = _combine(*_ce)

    e_cz = G.emit(lambda g: g.dma_start(zeroC, cz_d), signal=True)
    e_co = G.emit(lambda g: g.dma_start(ones_sb, co_d), signal=True)
    e_cm = G.emit(lambda g: g.dma_start(magic_sb, cm_d), signal=True)
    e_const = _combine(e_const, e_cz, e_co, e_cm)

    V.wait(e_const)
    V.wait(e_cz)
    V.wait(e_co)
    V.emit(lambda v: v.tensor_copy(eye_bf, eye_f))
    V.emit(lambda v: v.tensor_copy(eye_r, eye_f))
    V.emit(lambda v: v.tensor_copy(wb_r, wb_f))
    V.emit(lambda v: v.tensor_copy(wc_r, wc_f))
    V.emit(lambda v: v.tensor_copy(sc_r, zeroC))
    V.emit(lambda v: v.tensor_copy(zbf, zeroC[:, 0:8]))
    V.emit(lambda v: v.tensor_copy(onesp, zeroC[:, 0:128]))
    e_onesp = V.emit(lambda v: v.tensor_copy(onesp[0:16, :], ones_sb[0:16, 0:128]),
                     signal=True)
    e_eyebf = e_onesp

    S.wait(e_const)
    S.emit(lambda s: s.copy(fence_buf, zeroC[:, 0:8]))
    S.emitf(lambda s: s.activation(tmp16, a_sb, AF.Abs))
    S.emitf(lambda s: s.activation(tmp16, tmp16, AF.Exp, scale=-1.0))
    S.emitf(lambda s: s.activation(tmp16, tmp16, AF.Ln, bias=1.0))
    e_sp1 = S.emit(lambda s: s.activation(tmp16b, a_sb, AF.Relu), signal=True)
    V.wait(e_sp1)
    e_spv = V.emitf(lambda v: v.tensor_add(tmp16, tmp16, tmp16b))
    S.wait(e_spv)
    e_dec = S.emit(lambda s: s.activation(dec, tmp16, AF.Exp, scale=-1.0), signal=True)
    V.wait(e_dec)
    e_decbc = V.emit(lambda v: v.tensor_scalar_mul(dec_bc, ones_sc, dec), signal=True)

    tb = {"n": 0, "rd": [None, None]}

    def tb_next():
        i = tb["n"] % 2
        tb["n"] += 1
        return PB[6 + i], (6 + i), tb["rd"][i]

    def tb_reader(bank_idx, ev):
        tb["rd"][bank_idx - 6] = ev

    slab_ev = [None] * SLOTS_PER_CHUNK

    xload_ev = [None] * (CHUNKS + 2)
    xt_busy = [None, None]
    xrow_ev = [None] * (CHUNKS + 2)
    xrow_cast_ev = [None] * (CHUNKS + 2)
    xfm_ready = [None] * (CHUNKS + 2)
    xtm_free = [None] * (CHUNKS + 2)
    xfm_free = [None] * (CHUNKS + 2)

    def emit_x_load(i):
        b, ci = divmod(i, SEQCH)
        c0 = ci * C
        E = Q if i < 2 else G
        if i >= 2:
            E.wait(xtm_free[i - 2])
            E.wait(xrow_cast_ev[i - 2])
        _xe = []
        for tt in range(NT):
            _xe.append(E.emit(lambda g, b=b, c0=c0, tt=tt, i=i: g.dma_start(
                x_tm[:, i % 2, tt, :], x_d[b, c0 + tt * 128:c0 + (tt + 1) * 128, :]),
                signal=True))
        xload_ev[i] = _combine(*_xe)
        if ci != SEQCH - 1:
            xrow_ev[i] = E.emit(lambda g, b=b, c0=c0, i=i: g.dma_start(
                xrow_f[:, i % 2, :], x_d[b, c0 + C:c0 + C + 1, :]), signal=True)

    def emit_x_ingest(i):
        b, ci = divmod(i, SEQCH)
        seq_start = ci == 0
        seq_end = ci == SEQCH - 1
        pi = i % 2
        if i >= 2:
            V.wait(xfm_free[i - 2])
        if seq_start:
            V.emit(lambda v, pi=pi: v.tensor_copy(x_fm[:, pi, :, 0:1], zbf.unsqueeze(2)))
        else:
            V.emit(lambda v, pi=pi: v.tensor_copy(
                x_fm[:, pi, :, 0:1], x_fm[:, 1 - pi, :, 256:257]))
        S.wait(xload_ev[i])
        if xt_busy[0] is not None:
            S.wait(xt_busy[0])
        e_xtc = S.emit(lambda s, pi=pi: s.copy(xt_bf, x_tm[:, pi, :, :]),
                       signal=True)
        for tt in range(NT):
            for g4 in range(2):
                ps, bidx, last_rd = tb_next()
                T.wait(e_xtc)
                if i >= 2:
                    T.wait(xfm_free[i - 2])
                if last_rd is not None:
                    T.wait(last_rd)
                if i == 0:
                    T.wait(e_eyebf)
                for kk in range(4):
                    k = g4 * 4 + kk
                    e_tr = T.emit(lambda t, tt=tt, k=k, kk=kk, ps=ps: t.transpose(
                        ps.bitcast(BF16)[:, kk * 128:(kk + 1) * 128],
                        xt_bf[:, tt, k * 128:(k + 1) * 128], eye_bf),
                        signal=(kk == 3))
                V.wait(e_tr)
                e_ev = V.emit(lambda v, ps=ps, pi=pi, g4=g4, tt=tt: v.tensor_copy(
                    x_fm[:, pi, g4 * 4:(g4 + 1) * 4, 1 + tt * 128:1 + (tt + 1) * 128],
                    ps.bitcast(BF16).rearrange("p (a b) -> p a b", b=128)[:, 0:4, :]),
                    signal=True)
                tb_reader(bidx, e_ev)
        xt_busy[0] = e_tr
        if seq_end:
            ev_h = V.emit(lambda v, pi=pi: v.tensor_copy(
                x_fm[:, pi, :, W - 1:W], zbf.unsqueeze(2)), signal=True)
        else:
            ps, bidx, last_rd = tb_next()
            T.wait(xrow_ev[i])
            if last_rd is not None:
                T.wait(last_rd)
            for k in range(ND):
                e_tr = T.emit(lambda t, k=k, ps=ps, pi=pi: t.transpose(
                    ps[:, k:k + 1], xrow_f[0:1, pi, k * 128:(k + 1) * 128],
                    eye_f[0:1, 0:1]), signal=(k == ND - 1))
            xrow_cast_ev[i] = e_tr
            V.wait(e_tr)
            ev_h = V.emit(lambda v, ps=ps, pi=pi: v.tensor_copy(
                x_fm[:, pi, :, W - 1:W],
                ps.rearrange("p (a b) -> p a b", b=1)[:, 0:ND, :]), signal=True)
            tb_reader(bidx, ev_h)
        xfm_ready[i] = ev_h

    emit_x_load(0)
    emit_x_ingest(0)
    if n_chunks > 1:
        emit_x_load(1)
        emit_x_ingest(1)

    pre = {}
    win_load_ev = []
    for k in range(ND):
        ev = G.emit(lambda g, k=k: g.dma_start(
            krow_aps[k], win_d[k * 128:(k + 1) * 128, :]), signal=True)
        win_load_ev.append(ev)

    def emit_A_staging():
        last_cp = None
        for s_old in range(16):
            s = s_old if s_old < 8 else 40 + (s_old - 8)
            stg, sb = stg_next()
            if stg_state["busy"][sb] is not None:
                S.wait(stg_state["busy"][sb])
            for jj in range(2):
                j = 2 * s_old + jj
                for k in range(ND):
                    src = krow_aps[k][:, j * 128:(j + 1) * 128]
                    S.wait(win_load_ev[k])
                    tp = jj * 1024 + k * 128
                    last_cp = S.emit(lambda s_, src=src, tp=tp, stg=stg: s_.copy(
                        stg[:, tp:tp + 128], src),
                        signal=(jj == 1 and k == ND - 1))
            Q.wait(sfence())
            ev_s = Q.emit(lambda q, s=s, stg=stg: q.dma_start(ws_d[s], stg),
                          signal=True)
            slab_ev[s] = ev_s
            stg_stored(sb, ev_s)
        pre["a_last_cp"] = last_cp

    ring_load_ev = []
    slot_done_ev = []
    next_load = [0]
    want_load = [RING_SLOTS]
    wo_store_ev = {}
    wo_r3 = wo_d.rearrange("(k p) n -> p k n", p=128)

    def ring_tile(q, t):
        rs = q % RING_SLOTS
        return ring[:, rs * SLOT_T + t, :]

    WM = list(range(0, 8))
    S1 = list(range(8, 24))
    S2 = list(range(24, 40))
    WR = list(range(40, 48))
    WO = list(range(48, 56))
    slot_slabs = list(WO)
    for i in range(n_chunks):
        if i + 1 < n_chunks:
            slot_slabs += WM + WR
        slot_slabs += S1 + S2
        if i + 1 < n_chunks:
            slot_slabs += WO

    def pump_ring():
        while next_load[0] < min(want_load[0], len(slot_slabs)):
            q = next_load[0]
            s = slot_slabs[q]
            if q >= 8 and slab_ev[s] is None:
                return
            if q >= RING_SLOTS:
                Q.wait(slot_done_ev[q - RING_SLOTS])
                if q - RING_SLOTS in wo_store_ev:
                    Q.wait(wo_store_ev[q - RING_SLOTS])
            if q in (8, 9):
                Q.wait(pre["a_last_cp"])
                Q.wait(xfm_free[0])
            rs = q % RING_SLOTS
            if q < 8:
                dj = s - 48
                dst = ring[:, rs * SLOT_T:(rs + 1) * SLOT_T, :]
                srcp = wo_r3[:, :, dj * 128:(dj + 1) * 128]
                ev = G.emit(lambda g, dst=dst, srcp=srcp: g.dma_start(dst, srcp),
                            signal=True)
            else:
                Q.wait(slab_ev[s])
                dst = ring[:, rs * SLOT_T:(rs + 1) * SLOT_T, :].rearrange(
                    "p a b -> p (a b)")
                ev = Q.emit(lambda g, s=s, dst=dst: g.dma_start(dst, ws_d[s]),
                            signal=True)
            ring_load_ev.append(ev)
            next_load[0] += 1

    def emit_wo_stores():
        for q in range(8):
            s = 48 + q
            rs = q % RING_SLOTS
            srcp = ring[:, rs * SLOT_T:(rs + 1) * SLOT_T, :].rearrange(
                "p a b -> p (a b)")
            Q.wait(ring_load_ev[q])
            ev = Q.emit(lambda g, s=s, srcp=srcp: g.dma_start(ws_d[s], srcp),
                        signal=True)
            slab_ev[s] = ev
            wo_store_ev[q] = ev

    def emit_preproc_C(lo, hi):
        c_sgn = pre.setdefault("c_sgn", [None, None])
        c_abs = pre.setdefault("c_abs", [None, None])
        c_tr = pre.setdefault("c_tr", [None, None])
        for hj in range(lo, hi):
            hb = hj % 2
            wr = wrow_f[:, hb * DM:(hb + 1) * DM]
            sro = srow_bf[:, hb * DM:(hb + 1) * DM]
            if c_sgn[hb] is not None:
                G.wait(c_sgn[hb])
                G.wait(c_abs[hb])
            if hj == 0:
                G.wait(pre["a_last_cp"])
                G.wait(xfm_free[0])
            e_l = G.emit(lambda q, hj=hj, wr=wr: q.dma_start(
                wr, w1_d[hj * 128:(hj + 1) * 128, :]), signal=True)
            S.wait(e_l)
            if c_tr[hb] is not None:
                S.wait(c_tr[hb])
            e_sgn = S.emit(lambda s, sro=sro, wr=wr: s.sign(sro, wr), signal=True)
            V.wait(e_l)
            e_abs = V.emitf(lambda v, hj=hj, wr=wr: v.reduce_sum(
                al1[:, hj:hj + 1], wr, AX.X, apply_absolute_value=True))
            ps, bidx, last_rd = tb_next()
            psv = ps.bitcast(BF16)
            T.wait(e_sgn)
            if hj == 0:
                T.wait(e_eyebf)
            if last_rd is not None:
                T.wait(last_rd)
            for k in range(ND):
                e_tr = T.emit(lambda t, k=k, psv=psv, sro=sro: t.transpose(
                    psv[:, k * 128:(k + 1) * 128], sro[:, k * 128:(k + 1) * 128],
                    eye_bf), signal=(k == ND - 1))
            S.wait(e_tr)
            if hj % 2 == 0:
                cur = stg_next()
                stg_state["cur"] = cur
                if stg_state["busy"][cur[1]] is not None:
                    S.wait(stg_state["busy"][cur[1]])
            stg, sb = stg_state["cur"]
            e_ev = S.emit(lambda s_, psv=psv, off=(hj % 2) * 1024, stg=stg: s_.copy(
                stg[:, off:off + 1024], psv[:, 0:1024]), signal=True)
            tb_reader(bidx, e_ev)
            if hj % 2 == 1:
                s = 8 + hj // 2
                Q.wait(sfence())
                slab_ev[s] = Q.emit(lambda q, s=s, stg=stg: q.dma_start(ws_d[s], stg),
                                    signal=True)
                stg_stored(sb, slab_ev[s])
            c_sgn[hb] = e_sgn
            c_abs[hb] = e_abs
            c_tr[hb] = e_tr
        if hi == NH:
            pre["sgn"] = c_sgn[1]
            pre["abs"] = c_abs[1]
            pre["tr"] = c_tr[1]
            V.emitf(lambda v: v.tensor_scalar_mul(al1, al1, 1.0 / DM))
        pump_ring()

    def emit_preproc_D(lo, hi):
        prev_sgn, prev_abs, prev_tr = pre["sgn"], pre["abs"], pre["tr"]
        for dj in range(lo, hi):
            if prev_sgn is not None:
                G.wait(prev_sgn)
            if prev_abs is not None:
                G.wait(prev_abs)
            e_l = G.emit(lambda q, dj=dj: q.dma_start(
                wrow_f, w2_d[dj * 128:(dj + 1) * 128, :]), signal=True)
            S.wait(e_l)
            if prev_tr is not None:
                S.wait(prev_tr)
            e_sgn = S.emit(lambda s: s.sign(srow_bf, wrow_f), signal=True)
            V.wait(e_l)
            e_abs = V.emitf(lambda v, dj=dj: v.reduce_sum(
                al2[:, dj:dj + 1], wrow_f, AX.X, apply_absolute_value=True))
            for g4 in range(4):
                ps, bidx, last_rd = tb_next()
                psv = ps.bitcast(BF16)
                T.wait(e_sgn)
                if last_rd is not None:
                    T.wait(last_rd)
                for kk in range(8):
                    k = g4 * 8 + kk
                    e_tr = T.emit(lambda t, k=k, kk=kk, psv=psv: t.transpose(
                        psv[:, kk * 128:(kk + 1) * 128],
                        srow_bf[:, k * 128:(k + 1) * 128],
                        eye_bf), signal=(kk == 7))
                S.wait(e_tr)
                s = 24 + dj * 2 + g4 // 2
                if g4 % 2 == 0:
                    cur = stg_next()
                    stg_state["cur"] = cur
                    if stg_state["busy"][cur[1]] is not None:
                        S.wait(stg_state["busy"][cur[1]])
                stg, sb = stg_state["cur"]
                e_ev = S.emit(lambda s_, psv=psv, off=(g4 % 2) * 1024, stg=stg:
                              s_.copy(stg[:, off:off + 1024], psv[:, 0:1024]),
                              signal=True)
                tb_reader(bidx, e_ev)
                if g4 % 2 == 1:
                    Q.wait(sfence())
                    slab_ev[s] = Q.emit(lambda q, s=s, stg=stg: q.dma_start(
                        ws_d[s], stg), signal=True)
                    stg_stored(sb, slab_ev[s])
            prev_sgn = e_sgn
            prev_abs = e_abs
            prev_tr = e_tr
        pre["sgn"], pre["abs"], pre["tr"] = prev_sgn, prev_abs, prev_tr
        if hi == ND:
            V.emitf(lambda v: v.tensor_scalar_mul(al2, al2, 1.0 / DH))
        pump_ring()

    tile_ctr = [0]

    def mm_group(ps_view, rhs_of_k, n_k, extra_wait=None, wait_of_k=None):
        last_ev = None
        for k in range(n_k):
            g = tile_ctr[0]
            tile_ctr[0] += 1
            q, t = divmod(g, SLOT_T)
            ap = ring_tile(q, t)
            T.wait(ring_load_ev[q])
            if extra_wait is not None and k == 0:
                T.wait(extra_wait)
            if wait_of_k is not None:
                T.wait(wait_of_k(k))
            is_slot_end = (t == SLOT_T - 1)
            sig = is_slot_end or (k == n_k - 1)
            rhs = rhs_of_k(k)
            ev = T.emit(lambda t_, ps=ps_view, ap=ap, rhs=rhs,
                        st=(k == 0), sp=(k == n_k - 1):
                        t_.matmul(ps, ap, rhs, start=st, stop=sp), signal=sig)
            if is_slot_end:
                slot_done_ev.append(ev)
                want_load[0] += 1
                pump_ring()
            last_ev = ev
        return last_ev

    def mm_group_direct(ps_view, rhs_of_k, n_k, j, res, extra_wait=None):
        last_ev = None
        for k in range(n_k):
            ap = krow_aps[k][:, res * 2048 + j * 128:res * 2048 + (j + 1) * 128]
            T.wait(win_load_ev[k])
            if extra_wait is not None and k == 0:
                T.wait(extra_wait)
            rhs = rhs_of_k(k)
            ev = T.emit(lambda t_, ps=ps_view, ap=ap, rhs=rhs,
                        st=(k == 0), sp=(k == n_k - 1):
                        t_.matmul(ps, ap, rhs, start=st, stop=sp),
                        signal=(k == n_k - 1))
            last_ev = ev
        return last_ev

    ln = {"res_done": None, "last_apply": None, "snap": None,
          "apply_tt": [None] * NT}

    def ln_front(src_fm, res_tm, e_src):
        e_sq = None
        for tt in range(NT):
            for half in range(2):
                ps, bidx, last_rd = tb_next()
                T.wait(e_src(half) if callable(e_src) else e_src)
                if last_rd is not None:
                    T.wait(last_rd)
                for kk in range(4):
                    dj = half * 4 + kk
                    e_tr = T.emit(lambda t, tt=tt, dj=dj, kk=kk, ps=ps: t.transpose(
                        ps.bitcast(F32R)[:, kk * 128:(kk + 1) * 128],
                        src_fm[:, dj, tt * 128:(tt + 1) * 128], eye_r), signal=(kk == 3))
                V.wait(e_tr)
                e_r = V.emit(lambda v, ps=ps, tt=tt, half=half: v.scalar_tensor_tensor(
                    r_tm[:, tt, half * 512:(half + 1) * 512], ps, 1.0,
                    res_tm[:, tt, half * 512:(half + 1) * 512], ALU.mult, ALU.add,
                    accum_out=st_sum[:, half, tt:tt + 1]), signal=True)
                tb_reader(bidx, e_r)
                S.wait(e_r)
                e_sq = S.emit(lambda s, tt=tt, half=half: s.activation(
                    junk, r_tm[:, tt, half * 512:(half + 1) * 512], AF.Square,
                    accum_out=st_sq[:, half, tt:tt + 1]), signal=True)
        ln["res_done"] = e_r
        return e_r, e_sq

    def ln_tail(dst_tm, e_r, e_sq):
        GC.wait(e_r)
        GC.wait(e_sq)
        if ln["last_apply"] is not None:
            GC.wait(ln["last_apply"])
        GC.emit(lambda g: g.tensor_add(st_m, st_sum[:, 0, :], st_sum[:, 1, :]))
        GC.emit(lambda g: g.tensor_scalar_mul(st_m, st_m, 1.0 / DM))
        GC.emit(lambda g: g.tensor_add(st_var, st_sq[:, 0, :], st_sq[:, 1, :]))
        GC.emit(lambda g: g.tensor_scalar_mul(st_var, st_var, 1.0 / DM))
        GC.emit(lambda g: g.tensor_mul(st_t, st_m, st_m))
        GC.emit(lambda g: g.tensor_sub(st_var, st_var, st_t))
        e_u = GC.emit(lambda g: g.tensor_scalar_add(st_var, st_var, EPS), signal=True)
        V.wait(e_u)
        V.emitf(lambda v: v.tensor_scalar(
            st_inv.bitcast(mybir.dt.int32), st_var.bitcast(mybir.dt.int32),
            1, None, ALU.arith_shift_right))
        e_seed = V.emitf(lambda v: v.tensor_tensor(
            st_inv.bitcast(mybir.dt.int32), magic_sb[:, 0:NT],
            st_inv.bitcast(mybir.dt.int32), ALU.subtract))
        GC.wait(e_seed)
        for _ in range(3):
            GC.emit(lambda g: g.tensor_mul(st_t2, st_inv, st_inv))
            GC.emit(lambda g: g.tensor_mul(st_t2, st_t2, st_var))
            GC.emit(lambda g: g.tensor_scalar(st_t2, st_t2, -0.5, 1.5,
                                              ALU.mult, ALU.add))
            GC.emit(lambda g: g.tensor_mul(st_inv, st_inv, st_t2))
        GC.emit(lambda g: g.tensor_scalar(
            st_nmi, st_m, -1.0, None, ALU.mult))
        e_gst = GC.emit(lambda g: g.tensor_mul(st_nmi, st_nmi, st_inv), signal=True)
        V.wait(e_gst)
        e_dst = None
        for tt in range(NT):
            e_dst = V.emit(lambda v, tt=tt: v.tensor_scalar(
                dst_tm[:, tt, :], r_tm[:, tt, :],
                st_inv[:, tt:tt + 1], st_nmi[:, tt:tt + 1],
                ALU.mult, ALU.add), signal=True)
            ln["apply_tt"][tt] = e_dst
        ln["last_apply"] = e_dst
        return e_dst

    def ln_block(src_fm, res_tm, dst_tm, e_src):
        e_r, e_sq = ln_front(src_fm, res_tm, e_src)
        return ln_tail(dst_tm, e_r, e_sq)

    mb_reader = [None] * 4
    state = {"bc_reader": None, "ys_reader": None}
    out_store_ev = [None] * CHUNKS
    last_sil = [None] * (CHUNKS + 1)
    sil_evs = [[None] * NCH for _ in range(CHUNKS + 1)]
    y_evs = [[None] * NCH for _ in range(CHUNKS + 1)]
    at_evs = [[None] * ND for _ in range(CHUNKS + 1)]
    h_evs = [[None] * NH for _ in range(CHUNKS + 1)]
    ff_evs = [[None] * ND for _ in range(CHUNKS + 1)]
    e_scv_l = [None] * (CHUNKS + 1)
    e_y_l = [None] * (CHUNKS + 1)
    e_x1_l = [None] * (CHUNKS + 1)
    x1_apply = [None] * (CHUNKS + 1)
    e_at_l = [None] * (CHUNKS + 1)
    e_h_l = [None] * (CHUNKS + 1)
    e_ff_l = [None] * (CHUNKS + 1)
    e_ys_l = [None] * (CHUNKS + 1)

    def ph_M(i, direct=False):
        pi = i % 2
        pend = [None, None]

        def emit_silu(pj, e_c3):
            S.wait(e_c3)
            ev = S.emit(lambda s, pj=pj, bank=pj % 4: s.activation(
                xc[:, pj, :], cv2[:, bank, :], AF.Silu, bias=cb_sb[:, pj:pj + 1]),
                signal=True)
            sil_evs[i][pj] = ev
            last_sil[i] = ev

        for j in range(NCH):
            bank = j % 4
            ps = PB[bank]
            if mb_reader[bank] is not None:
                T.wait(mb_reader[bank])
            if direct:
                e_mm = mm_group_direct(
                    ps[:, 0:W], lambda k, pi=pi: x_fm[:, pi, k, :], ND, j=j,
                    res=0, extra_wait=xfm_ready[i] if j == 0 else None)
            else:
                e_mm = mm_group(ps[:, 0:W], lambda k, pi=pi: x_fm[:, pi, k, :], ND,
                                extra_wait=xfm_ready[i] if j == 0 else None)
            S.wait(e_mm)
            e_c1 = S.emit(lambda s, ps=ps, j=j, bank=bank: s.activation(
                cv1[:, bank, :], ps[:, 0:C], AF.Identity, scale=wtap[:, j, 0:1]),
                signal=True)
            V.wait(e_c1)
            V.emit(lambda v, ps=ps, j=j, bank=bank: v.scalar_tensor_tensor(
                cv2[:, bank, :], ps[:, 1:C + 1], wtap[:, j, 1:2], cv1[:, bank, :],
                ALU.mult, ALU.add))
            e_c3 = V.emit(lambda v, ps=ps, j=j, bank=bank: v.scalar_tensor_tensor(
                cv2[:, bank, :], ps[:, 2:C + 2], wtap[:, j, 2:3], cv2[:, bank, :],
                ALU.mult, ALU.add), signal=True)
            mb_reader[bank] = e_c3
            if pend[0] is not None:
                emit_silu(*pend[0])
            pend[0] = pend[1]
            pend[1] = (j, e_c3)
        for p in pend:
            if p is not None:
                emit_silu(*p)

    def ph_BCscan(i):
        b, ci = divmod(i, SEQCH)
        seq_start = ci == 0
        if state["bc_reader"] is not None:
            T.wait(state["bc_reader"])
        e_bc = None
        for j in range(NCH):
            T.wait(sil_evs[i][j])
            T.emit(lambda t, j=j, st=(j == 0): t.matmul(
                bc_ps[0:16, 0:C], wb_r[:, j, :], xc[:, j, :], start=st, stop=False))
            e_bc = T.emit(lambda t, j=j, sig=(j == NCH - 1): t.matmul(
                bc_ps[0:16, C:2 * C], wc_r[:, j, :], xc[:, j, :],
                start=False, stop=sig), signal=(j == NCH - 1))
        if seq_start:
            V.emit(lambda v: v.tensor_copy(carry, zeroC[0:16, 0:1]))
        V.wait(e_bc)
        V.wait(e_decbc)
        V.emitf(lambda v: v.tensor_tensor_scan(
            s_sc, dec_bc, bc_ps[0:16, 0:C], carry, ALU.mult, ALU.add))
        V.emitf(lambda v: v.tensor_copy(carry, s_sc[:, C - 1:C]))
        e_scv_l[i] = V.emit(lambda v: v.tensor_tensor(
            sc_r[0:16, :], s_sc, bc_ps[0:16, C:2 * C], ALU.mult), signal=True)
        state["bc_reader"] = e_scv_l[i]

    def ph_ysR(i, direct=False):
        pi = i % 2
        if state["ys_reader"] is not None:
            T.wait(state["ys_reader"])
        T.wait(e_scv_l[i])
        T.wait(e_onesp)
        e_ys = T.emit(lambda t: t.matmul(ys_ps, onesp, sc_r, start=True, stop=True),
                      signal=True)
        e_ys_l[i] = e_ys
        e_y = None
        for j in range(NCH):
            bank = j % 4
            ps = PB[bank]
            if mb_reader[bank] is not None:
                T.wait(mb_reader[bank])
            if direct:
                e_mm = mm_group_direct(
                    ps[:, 0:C], lambda k, pi=pi: x_fm[:, pi, k, 1:C + 1], ND,
                    j=j, res=1)
            else:
                e_mm = mm_group(ps[:, 0:C],
                                lambda k, pi=pi: x_fm[:, pi, k, 1:C + 1], ND)
            if j == NCH - 1:
                xfm_free[i] = e_mm
            S.wait(e_mm)
            e_sr = S.emit(lambda s, ps=ps, bank=bank: s.activation(
                sr[:, bank, :], ps[:, 0:C], AF.Silu), signal=True)
            V.wait(e_ys)
            V.wait(e_sr)
            V.emit(lambda v, j=j, bank=bank: v.scalar_tensor_tensor(
                t2[:, bank, :], xc_f[:, j, :], dv_sb[:, j:j + 1], ys_ps,
                ALU.mult, ALU.add))
            e_y = V.emit(lambda v, j=j, bank=bank: v.tensor_tensor(
                y_bf[:, j, :], t2[:, bank, :], sr[:, bank, :], ALU.mult), signal=True)
            mb_reader[bank] = e_y
            y_evs[i][j] = e_y
        S.emit(lambda s: s.activation(junk[0:1, 0:1], fence_buf[0:1, 0:1], AF.Gelu))
        e_y_l[i] = e_y
        state["ys_reader"] = e_y

    def ph_W(i):
        for dj in range(ND):
            bank = dj % 4
            ps = PB[bank]
            if mb_reader[bank] is not None:
                T.wait(mb_reader[bank])
            e_mm = mm_group(ps[:, 0:C], lambda k: y_bf[:, k, :], NCH,
                            wait_of_k=lambda k, i=i: y_evs[i][k])
            S.wait(e_mm)
            e_at = S.emit(lambda s, ps=ps, dj=dj: s.activation(
                attn_sb[:, dj, :], ps[:, 0:C], AF.Identity), signal=True)
            mb_reader[bank] = e_at
            at_evs[i][dj] = e_at

    def ph_T1_front(i):
        pi = i % 2
        ev = ln_front(attn_sb, x_tm[:, pi, :, :],
                      lambda half, i=i: at_evs[i][half * 4 + 3])
        xtm_free[i] = ln["res_done"]
        return ev

    def ph_T1_tail(i, ev):
        e_x1_l[i] = ln_tail(x1_tm, *ev)
        x1_apply[i] = list(ln["apply_tt"])

    def ph_T2(i):
        e_x1fm = None
        x1f_done = {}
        for tt in range(NT):
            for g4 in range(2):
                ps, bidx, last_rd = tb_next()
                T.wait(x1_apply[i][tt])
                if last_rd is not None:
                    T.wait(last_rd)
                for kk in range(4):
                    k = g4 * 4 + kk
                    e_tr = T.emit(lambda t, tt=tt, k=k, kk=kk, ps=ps: t.transpose(
                        ps.bitcast(F32R)[:, kk * 128:(kk + 1) * 128],
                        x1_tm[:, tt, k * 128:(k + 1) * 128], eye_r), signal=(kk == 3))
                V.wait(e_tr)
                e_x1fm = V.emit(lambda v, ps=ps, g4=g4, tt=tt: v.tensor_copy(
                    x1_fm[:, g4 * 4:(g4 + 1) * 4, tt * 128:(tt + 1) * 128],
                    ps.rearrange("p (a b) -> p a b", a=4)), signal=True)
                tb_reader(bidx, e_x1fm)
                if tt == NT - 1:
                    x1f_done[g4] = e_x1fm
        return x1f_done

    def ph_H(i, x1f_done):
        e_h = None
        for hj in range(NH):
            bank = hj % 4
            ps = PB[bank]
            if mb_reader[bank] is not None:
                T.wait(mb_reader[bank])
            e_mm = mm_group(ps[:, 0:C], lambda k: x1_fm[:, k, :], ND,
                            wait_of_k=lambda k: x1f_done[k // 4])
            S.wait(e_mm)
            e_h = S.emit(lambda s, ps=ps, hj=hj: s.activation(
                h_bf[:, hj, :], ps[:, 0:C], AF.Gelu,
                bias=b1_sb[:, hj:hj + 1], scale=al1[:, hj:hj + 1]), signal=True)
            mb_reader[bank] = e_h
            h_evs[i][hj] = e_h
        S.emit(lambda s: s.activation(junk[0:1, 0:1], fence_buf[0:1, 0:1], AF.Silu))
        e_h_l[i] = e_h

    def ph_O(i):
        for dj in range(ND):
            bank = dj % 4
            ps = PB[bank]
            if mb_reader[bank] is not None:
                T.wait(mb_reader[bank])
            e_mm = mm_group(ps[:, 0:C], lambda k: h_bf[:, k, :], NH,
                            wait_of_k=lambda k, i=i: h_evs[i][k])
            S.wait(e_mm)
            e_ff = S.emit(lambda s, ps=ps, dj=dj: s.activation(
                attn_sb[:, dj, :], ps[:, 0:C], AF.Identity,
                bias=b2_sb[:, dj:dj + 1], scale=al2[:, dj:dj + 1]), signal=True)
            mb_reader[bank] = e_ff
            ff_evs[i][dj] = e_ff

    def ph_T3(i):
        b, ci = divmod(i, SEQCH)
        c0 = ci * C
        pi = i % 2
        if i >= 2 and out_store_ev[i - 2] is not None:
            V.wait(out_store_ev[i - 2])
        e_out = ln_block(attn_sb, x1_f, out_tm[:, pi, :, :],
                         lambda half, i=i: ff_evs[i][half * 4 + 3])
        _oe = []
        for tt in range(NT):
            G.wait(ln["apply_tt"][tt])
            _oe.append(G.emit(lambda g, b=b, c0=c0, tt=tt, pi=pi: g.dma_start(
                out_d[b, c0 + tt * 128:c0 + (tt + 1) * 128, :], out_tm[:, pi, tt, :]),
                signal=True))
        out_store_ev[i] = _combine(*_oe)

    def emit_debug(plimit=99):
        if plimit >= 2:
            e_l = V.emit(lambda v: v.tensor_copy(junk[0:16, 0:2 * C], bc_ps[0:16, :]),
                         signal=True)
            G.wait(e_l)
            G.emit(lambda g: g.dma_start(dbg_outs["d_bc"], junk[0:16, 0:2 * C]),
                   signal=True)
            G.emit(lambda g: g.dma_start(dbg_outs["d_ssc"], s_sc), signal=True)
            G.emit(lambda g: g.dma_start(dbg_outs["d_scr"], sc_r), signal=True)
            G.emit(lambda g: g.dma_start(dbg_outs["d_dec"], dec_bc), signal=True)
            G.emit(lambda g: g.dma_start(dbg_outs["d_ones"], onesp), signal=True)
        if plimit >= 2:
            e_l = V.emit(lambda v: v.tensor_copy(junk[:, 0:C], ys_ps), signal=True)
            G.wait(e_l)
            G.emit(lambda g: g.dma_start(dbg_outs["d_ys"], junk[:, 0:C]), signal=True)
        pairs = [("d_al1", al1, 0), ("d_b1", b1_sb, 0), ("d_al2", al2, 0),
                 ("d_x1fm", x1_fm, 6),
                 ("d_xc", xc, 1), ("d_x1", x1_tm, 5), ("d_h", h_bf, 7),
                 ("d_y", y_bf, 3), ("d_ffn", attn_sb, 8), ("d_r", r_tm, 4),
                 ("d_stm", st_m, 4), ("d_stv", st_var, 4), ("d_sti", st_inv, 4),
                 ("d_ssum", st_sum, 4), ("d_ssq", st_sq, 4)]
        G.wait(_Ev("s_v", V.count))
        G.wait(_Ev("s_s", S.count))
        for nm, srcap, need in pairs:
            if plimit >= need:
                G.emit(lambda g, nm=nm, srcap=srcap: g.dma_start(dbg_outs[nm], srcap),
                       signal=True)

    ph_M(0, direct=True)
    ph_BCscan(0)
    emit_A_staging()
    ph_ysR(0, direct=True)
    pump_ring()
    emit_wo_stores()
    ph_W(0)
    for i in range(n_chunks):
        ev_t1 = ph_T1_front(i)
        if i == 0:
            emit_preproc_C(0, 8)
        if i + 1 < n_chunks:
            ph_M(i + 1)
        if i == 0:
            emit_preproc_C(8, 16)
        if i + 1 < n_chunks:
            ph_BCscan(i + 1)
        if i + 2 < n_chunks:
            emit_x_load(i + 2)
            emit_x_ingest(i + 2)
        ph_T1_tail(i, ev_t1)
        if i == 0:
            emit_preproc_C(16, 24)
        if i + 1 < n_chunks:
            ph_ysR(i + 1)
        if i == 0:
            emit_preproc_C(24, 32)
            emit_preproc_D(0, 4)
        x1f_done = ph_T2(i)
        if i == 0:
            emit_preproc_D(4, 8)
        ph_H(i, x1f_done)
        ph_O(i)
        ph_T3(i)
        if i + 1 < n_chunks:
            ph_W(i + 1)

    if debug == "ws":
        for j in range(Q.NPOOL):
            n_j = (Q.ndma - j + Q.NPOOL - 1) // Q.NPOOL
            if n_j:
                G.wait(_Ev(f"{Q.semname}{j}", 16 * n_j))
        for s in range(SLOTS_PER_CHUNK):
            G.emit(lambda g, s=s: g.dma_start(wsdump[s], ws_d[s]), signal=True)
    elif debug:
        emit_debug(DEBUG_PHASE_LIMIT[0])

    import contextlib
    with contextlib.ExitStack() as ctx:
        for eng in (T, V, S, Q, G, GC):
            for nm in eng.semnames():
                p.sems[nm] = ctx.enter_context(nc.semaphore(nm))
        with nc.Block() as blk:
            blk.tensor(lambda e: [op(e) for op in T.ops])
            blk.vector(lambda e: [op(e) for op in V.ops])
            blk.scalar(lambda e: [op(e) for op in S.ops])
            blk.sync(lambda e: [op(e) for op in Q.ops])
            blk.gpsimd(lambda e: [op(e) for op in G.ops])
    return nc


DEBUG_PHASE_LIMIT = [99]


def debug_phase_limit(i, n_chunks):
    return DEBUG_PHASE_LIMIT[0] if i == n_chunks - 1 else 99


_NC_CACHE = {}


def kernel(L0_attn_x, L0_attn_W_in, L0_attn_conv_w, L0_attn_conv_b, L0_attn_W_B,
           L0_attn_W_C, L0_attn_A, L0_attn_D, L0_attn_W_out, L0_ffn_w1, L0_ffn_b1,
           L0_ffn_w2, L0_ffn_b2):
    x = np.ascontiguousarray(np.asarray(L0_attn_x, dtype=np.float32))
    common = dict(
        W_in=np.ascontiguousarray(np.asarray(L0_attn_W_in, np.float32)),
        conv_w=np.ascontiguousarray(np.asarray(L0_attn_conv_w, np.float32).reshape(DI, 3)),
        conv_b=np.ascontiguousarray(np.asarray(L0_attn_conv_b, np.float32)),
        W_B=np.ascontiguousarray(np.asarray(L0_attn_W_B, np.float32)),
        W_C=np.ascontiguousarray(np.asarray(L0_attn_W_C, np.float32)),
        A=np.ascontiguousarray(np.asarray(L0_attn_A, np.float32)),
        Dv=np.ascontiguousarray(np.asarray(L0_attn_D, np.float32)),
        W_out=np.ascontiguousarray(np.asarray(L0_attn_W_out, np.float32)),
        w1=np.ascontiguousarray(np.asarray(L0_ffn_w1, np.float32)),
        b1=np.ascontiguousarray(np.asarray(L0_ffn_b1, np.float32)),
        w2=np.ascontiguousarray(np.asarray(L0_ffn_w2, np.float32)),
        b2=np.ascontiguousarray(np.asarray(L0_ffn_b2, np.float32)),
        eye=np.eye(128, dtype=np.float32),
        czeros=np.zeros((128, 256), dtype=np.float32),
        cones=np.ones((128, 256), dtype=np.float32),
        cmagic=np.full((128, 8), 0x5F3759DF, dtype=np.int32),
    )
    if "nc" not in _NC_CACHE:
        _NC_CACHE["nc"] = build_program()
    nc = _NC_CACHE["nc"]
    in_maps = []
    for c in range(NCORE):
        m = dict(common)
        m["x"] = np.ascontiguousarray(x[c * BPC:(c + 1) * BPC])
        in_maps.append(m)
    res = run_bass_kernel_spmd(nc, in_maps, core_ids=list(range(NCORE)))
    return np.concatenate([res.results[c]["out"] for c in range(NCORE)], axis=0)


# version 29
# speedup vs baseline: 1.0024x; 1.0024x over previous
import numpy as np
import concourse.bass as bass
import concourse.mybir as mybir
from concourse.bass_utils import run_bass_kernel_spmd

F32 = mybir.dt.float32
F32R = mybir.dt.float32r
BF16 = mybir.dt.bfloat16
AF = mybir.ActivationFunctionType
ALU = mybir.AluOpType
AX = mybir.AxisListType

NCORE = 8
BPC = 2
SEQ = 2048
DM = 1024
DI = 2048
DH = 4096
DS = 16
C = 256
W = C + 2
NT = C // 128
ND = DM // 128
NCH = DI // 128
NH = DH // 128
SEQCH = SEQ // C
CHUNKS = BPC * SEQCH

T_WIN = 2 * NCH * ND
T_WOUT = ND * NCH
T_S1 = NH * ND
T_S2 = ND * NH
TILES = T_WIN + T_WOUT + T_S1 + T_S2
SLOT_T = 16
SLOTS_PER_CHUNK = TILES // SLOT_T
RING_SLOTS = 11
EPS = 1e-5


class _Ev:
    __slots__ = ("sem", "val")

    def __init__(self, sem, val):
        self.sem = sem
        self.val = val


def _combine(*evs):
    out = []
    for e in evs:
        if e is None:
            continue
        if isinstance(e, list):
            out.extend(e)
        else:
            out.append(e)
    return out


class _Eng:

    NPOOL = 16

    def __init__(self, prog, semname, dma=False):
        self.prog = prog
        self.semname = semname
        self.dma = dma
        self.ops = []
        self.count = 0
        self.ndma = 0
        self.waited = {}

    def semnames(self):
        if not self.dma:
            return [self.semname]
        return [f"{self.semname}{j}" for j in range(self.NPOOL)]

    def wait(self, ev):
        if ev is None:
            return
        if isinstance(ev, list):
            for e in ev:
                self.wait(e)
            return
        if ev.sem == self.semname and not self.dma:
            return
        if self.waited.get(ev.sem, -1) >= ev.val:
            return
        self.waited[ev.sem] = ev.val
        sem, val = ev.sem, ev.val
        self.ops.append(lambda e, sem=sem, val=val: e.wait_ge(self.prog.sems[sem], val))

    def wait_own(self, ev):
        if ev is None:
            return
        if isinstance(ev, list):
            for e in ev:
                self.wait_own(e)
            return
        if self.waited.get(ev.sem, -1) >= ev.val:
            return
        self.waited[ev.sem] = ev.val
        sem, val = ev.sem, ev.val
        self.ops.append(lambda e, sem=sem, val=val: e.wait_ge(self.prog.sems[sem], val))

    def emitf(self, fn):
        ev = self.emit(fn, signal=True)
        self.waited.pop(ev.sem, None)
        self.wait_own(ev)
        return ev

    def emit(self, fn, signal=False):
        if self.dma:
            i = self.ndma
            self.ndma += 1
            j = i % self.NPOOL
            k = i // self.NPOOL
            sem = f"{self.semname}{j}"
            if k >= 1:
                self.wait(_Ev(sem, 16 * k))
            ev = _Ev(sem, 16 * (k + 1))

            def wrapped(e, fn=fn, sem=sem):
                fn(e).then_inc(self.prog.sems[sem], 16)
            self.ops.append(wrapped)
            return ev
        ev = None
        if signal:
            self.count += 1
            ev = _Ev(self.semname, self.count)

            def wrapped(e, fn=fn):
                fn(e).then_inc(self.prog.sems[self.semname], 1)
            self.ops.append(wrapped)
        else:
            self.ops.append(fn)
        return ev


class _Prog:
    def __init__(self):
        self.sems = {}


def _mk_fence(eng, fence_buf):
    def fence():
        eng.emitf(lambda e: e.tensor_copy(fence_buf[:, 0:4], fence_buf[:, 4:8]))
        return eng.emitf(lambda e: e.tensor_copy(fence_buf[:, 4:8], fence_buf[:, 0:4]))
    return fence


def build_program(debug=False, n_chunks=CHUNKS):
    nc = bass.Bass("TRN2", target_bir_lowering=False, detect_race_conditions=False)

    x_d = nc.dram_tensor("x", [BPC, SEQ, DM], F32, kind="ExternalInput").ap()
    win_d = nc.dram_tensor("W_in", [DM, 2 * DI], F32, kind="ExternalInput").ap()
    cw_d = nc.dram_tensor("conv_w", [DI, 3], F32, kind="ExternalInput").ap()
    cb_d = nc.dram_tensor("conv_b", [DI], F32, kind="ExternalInput").ap()
    wb_d = nc.dram_tensor("W_B", [DI, DS], F32, kind="ExternalInput").ap()
    wc_d = nc.dram_tensor("W_C", [DI, DS], F32, kind="ExternalInput").ap()
    a_d = nc.dram_tensor("A", [DS], F32, kind="ExternalInput").ap()
    dv_d = nc.dram_tensor("Dv", [DI], F32, kind="ExternalInput").ap()
    wo_d = nc.dram_tensor("W_out", [DI, DM], F32, kind="ExternalInput").ap()
    w1_d = nc.dram_tensor("w1", [DH, DM], F32, kind="ExternalInput").ap()
    b1_d = nc.dram_tensor("b1", [DH], F32, kind="ExternalInput").ap()
    w2_d = nc.dram_tensor("w2", [DM, DH], F32, kind="ExternalInput").ap()
    b2_d = nc.dram_tensor("b2", [DM], F32, kind="ExternalInput").ap()
    eye_d = nc.dram_tensor("eye", [128, 128], F32, kind="ExternalInput").ap()
    cz_d = nc.dram_tensor("czeros", [128, 256], F32, kind="ExternalInput").ap()
    co_d = nc.dram_tensor("cones", [128, 256], F32, kind="ExternalInput").ap()
    cm_d = nc.dram_tensor("cmagic", [128, 8], mybir.dt.int32, kind="ExternalInput").ap()
    out_d = nc.dram_tensor("out", [BPC, SEQ, DM], F32, kind="ExternalOutput").ap()
    ws_d = nc.dram_tensor("wstream", [SLOTS_PER_CHUNK, 128, SLOT_T * 128], BF16).ap()

    A_ = nc.alloc_sbuf_tensor
    ring = A_("ring", [128, RING_SLOTS * SLOT_T, 128], BF16).ap()
    x_tm = A_("x_tm", [128, 2, NT, DM], F32).ap()
    xrow_f = A_("xrow_f", [1, 2, DM], F32).ap()

    x_fm = A_("x_fm", [128, 2, ND, W], BF16).ap()
    xc = A_("xc", [128, NCH, C], F32R).ap()
    xc_f = xc.bitcast(F32)
    cv1 = A_("cv1", [128, 4, C], F32).ap()
    cv2 = A_("cv2", [128, 4, C], F32).ap()
    sr = A_("sr", [128, 4, C], F32).ap()

    y_bf = A_("y_bf", [128, NCH, C], BF16).ap()
    t2 = cv2
    s_sc = A_("s_sc", [16, C], F32).ap()
    carry = A_("carry", [16, 1], F32).ap()
    sc_r = A_("sc_r", [128, C], F32R).ap()
    onesp = A_("onesp", [128, 128], F32R).ap()
    zeroC = A_("zeroC", [128, C], F32).ap()
    attn_sb = A_("attn_sb", [128, ND, C], F32R).ap()
    attn_f = attn_sb.bitcast(F32)
    r_tm = A_("r_tm", [128, NT, DM], F32).ap()
    x1_tm = A_("x1_tm", [128, NT, DM], F32R).ap()
    x1_f = x1_tm.bitcast(F32)
    x1_fm = A_("x1_fm", [128, ND, C], BF16).ap()
    h_bf = A_("h_bf", [128, NH, C], BF16).ap()
    out_tm = A_("out_tm", [128, 2, NT, DM], F32).ap()
    junk = A_("junk", [128, 512], F32).ap()
    eye_f = A_("eye_f", [128, 128], F32).ap()
    eye_bf = A_("eye_bf", [128, 128], BF16).ap()
    eye_r = A_("eye_r", [128, 128], F32R).ap()
    wtap = A_("wtap", [128, NCH, 3], F32).ap()
    cb_sb = A_("cb_sb", [128, NCH], F32).ap()
    dv_sb = A_("dv_sb", [128, NCH], F32).ap()
    wb_f = A_("wb_f", [128, NCH, DS], F32).ap()
    wc_f = A_("wc_f", [128, NCH, DS], F32).ap()
    wb_r = A_("wb_r", [128, NCH, DS], F32R).ap()
    wc_r = A_("wc_r", [128, NCH, DS], F32R).ap()
    a_sb = A_("a_sb", [16, 1], F32).ap()
    dec = A_("dec", [16, 1], F32).ap()
    dec_bc = A_("dec_bc", [16, C], F32).ap()
    ones_sb = A_("ones_sb", [128, 256], F32).ap()
    ones_sc = ones_sb[0:16, :]
    zbf = A_("zbf", [128, 8], BF16).ap()
    magic_sb = A_("magic_sb", [128, 8], mybir.dt.int32).ap()
    tmp16 = A_("tmp16", [16, 1], F32).ap()
    tmp16b = A_("tmp16b", [16, 1], F32).ap()
    b1_sb = A_("b1_sb", [128, NH], F32).ap()
    al1 = A_("al1", [128, NH], F32).ap()
    b2_sb = A_("b2_sb", [128, ND], F32).ap()
    al2 = A_("al2", [128, ND], F32).ap()
    st_sum = A_("st_sum", [128, 2, NT], F32).ap()
    st_sq = A_("st_sq", [128, 2, NT], F32).ap()
    st_m = A_("st_m", [128, NT], F32).ap()
    st_var = A_("st_var", [128, NT], F32).ap()
    st_inv = A_("st_inv", [128, NT], F32).ap()
    st_nmi = A_("st_nmi", [128, NT], F32).ap()
    st_t = A_("st_t", [128, NT], F32).ap()
    st_t2 = A_("st_t2", [128, NT], F32).ap()
    xt_bf = A_("xt_bf", [128, NT, 1024], BF16).ap()
    stg2 = A_("stg2", [128, 2, SLOT_T * 128], BF16).ap()
    stg_state = {"n": 0, "busy": [None, None], "cur": None}

    def stg_next():
        b = stg_state["n"] % 2
        stg_state["n"] += 1
        return stg2[:, b, :], b

    def stg_stored(b, ev):
        stg_state["busy"][b] = ev
    srow_bf = A_("srow_bf", [128, DH], BF16).ap()
    kr_bf = h_bf.rearrange("p a b -> p (a b)")
    out_flat = out_tm.rearrange("p a b c -> p (a b c)")
    kr2_bf = out_flat.bitcast(BF16)
    wrow_f = out_flat
    krow_aps = [
        kr_bf[:, 0:4096],
        kr_bf[:, 4096:8192],
        kr2_bf[:, 0:4096],
        kr2_bf[:, 4096:8192],
        srow_bf,
        r_tm.rearrange("p a b -> p (a b)").bitcast(BF16),
        ring[:, 8 * SLOT_T:10 * SLOT_T, :].rearrange("p a b -> p (a b)"),
        y_bf.rearrange("p a b -> p (a b)"),
    ]

    PB = [nc.alloc_psum_tensor(f"pb{i}", [128, 512], F32).ap() for i in range(8)]
    bc_ps = PB[4]
    ys_ps = PB[5][:, 0:C]

    fence_buf = A_("fence_buf", [128, 8], F32).ap()

    p = _Prog()
    T = _Eng(p, "s_pe")
    V = _Eng(p, "s_v")
    S = _Eng(p, "s_s")
    Q = _Eng(p, "s_ring", dma=True)
    G = _Eng(p, "s_g", dma=True)

    GC = _Eng(p, "s_gc")
    GC.ops = G.ops

    vfence = _mk_fence(V, fence_buf)

    def sfence():
        S.emitf(lambda e: e.copy(fence_buf[:, 0:4], fence_buf[:, 4:8]))
        return S.emitf(lambda e: e.copy(fence_buf[:, 4:8], fence_buf[:, 0:4]))

    dbg_outs = {}
    if debug == "ws":
        wsdump = nc.dram_tensor("wsdump", [SLOTS_PER_CHUNK, 128, SLOT_T * 128], BF16,
                                kind="ExternalOutput").ap()
    if debug and debug != "ws":
        for nm, shp, dt_ in [("d_xc", [128, NCH, C], F32R), ("d_ys", [128, C], F32),
                             ("d_x1", [128, NT, DM], F32), ("d_h", [128, NH, C], BF16),
                             ("d_y", [128, NCH, C], BF16), ("d_ffn", [128, ND, C], F32),
                             ("d_r", [128, NT, DM], F32)]:
            dbg_outs[nm] = nc.dram_tensor(nm, shp, dt_, kind="ExternalOutput").ap()
        for nm, shp, dt_ in [("d_ssc", [16, C], F32), ("d_scr", [128, C], F32R),
                             ("d_bc", [16, 2 * C], F32), ("d_dec", [16, C], F32),
                             ("d_ones", [128, 128], F32R), ("d_ys2", [128, NT, DM], F32),
                             ("d_snap", [128, 2, NT], F32), ("d_snap2", [128, 2, NT], F32),
                             ("d_snap3", [128, 4 * NT], F32),
                             ("d_al1", [128, NH], F32), ("d_b1", [128, NH], F32),
                             ("d_al2", [128, ND], F32), ("d_x1fm", [128, ND, C], BF16),
                             ("d_stm", [128, NT], F32), ("d_stv", [128, NT], F32),
                             ("d_sti", [128, NT], F32), ("d_ssum", [128, 2, NT], F32),
                             ("d_ssq", [128, 2, NT], F32)]:
            dbg_outs[nm] = nc.dram_tensor(nm, shp, dt_, kind="ExternalOutput").ap()

    def _slow(g, dst, srcap):
        return g.dma_start(dst, srcap, allow_slow_non_contiguous=True)

    _ce = []
    e_eye = G.emit(lambda g: g.dma_start(eye_f, eye_d), signal=True)
    _ce.append(e_eye)
    _ce.append(G.emit(lambda g: _slow(g, wtap, cw_d.rearrange("(j p) t -> p j t", p=128))))
    _ce.append(G.emit(lambda g: _slow(g, cb_sb, cb_d.rearrange("(j p) -> p j", p=128))))
    _ce.append(G.emit(lambda g: _slow(g, dv_sb, dv_d.rearrange("(j p) -> p j", p=128))))
    _ce.append(G.emit(lambda g: _slow(g, wb_f, wb_d.rearrange("(j p) s -> p j s", p=128))))
    _ce.append(G.emit(lambda g: _slow(g, wc_f, wc_d.rearrange("(j p) s -> p j s", p=128))))
    _ce.append(G.emit(lambda g: _slow(g, a_sb, a_d.rearrange("(s o) -> s o", o=1))))
    _ce.append(G.emit(lambda g: _slow(g, b1_sb, b1_d.rearrange("(j p) -> p j", p=128))))
    _ce.append(G.emit(lambda g: _slow(g, b2_sb, b2_d.rearrange("(j p) -> p j", p=128))))
    e_const = _combine(*_ce)

    e_cz = G.emit(lambda g: g.dma_start(zeroC, cz_d), signal=True)
    e_co = G.emit(lambda g: g.dma_start(ones_sb, co_d), signal=True)
    e_cm = G.emit(lambda g: g.dma_start(magic_sb, cm_d), signal=True)
    e_const = _combine(e_const, e_cz, e_co, e_cm)

    V.wait(e_const)
    V.wait(e_cz)
    V.wait(e_co)
    V.emit(lambda v: v.tensor_copy(eye_bf, eye_f))
    V.emit(lambda v: v.tensor_copy(eye_r, eye_f))
    V.emit(lambda v: v.tensor_copy(wb_r, wb_f))
    V.emit(lambda v: v.tensor_copy(wc_r, wc_f))
    V.emit(lambda v: v.tensor_copy(sc_r, zeroC))
    V.emit(lambda v: v.tensor_copy(zbf, zeroC[:, 0:8]))
    V.emit(lambda v: v.tensor_copy(onesp, zeroC[:, 0:128]))
    e_onesp = V.emit(lambda v: v.tensor_copy(onesp[0:16, :], ones_sb[0:16, 0:128]),
                     signal=True)
    e_eyebf = e_onesp

    S.wait(e_const)
    S.emit(lambda s: s.copy(fence_buf, zeroC[:, 0:8]))
    S.emitf(lambda s: s.activation(tmp16, a_sb, AF.Abs))
    S.emitf(lambda s: s.activation(tmp16, tmp16, AF.Exp, scale=-1.0))
    S.emitf(lambda s: s.activation(tmp16, tmp16, AF.Ln, bias=1.0))
    e_sp1 = S.emit(lambda s: s.activation(tmp16b, a_sb, AF.Relu), signal=True)
    V.wait(e_sp1)
    e_spv = V.emitf(lambda v: v.tensor_add(tmp16, tmp16, tmp16b))
    S.wait(e_spv)
    e_dec = S.emit(lambda s: s.activation(dec, tmp16, AF.Exp, scale=-1.0), signal=True)
    V.wait(e_dec)
    e_decbc = V.emit(lambda v: v.tensor_scalar_mul(dec_bc, ones_sc, dec), signal=True)

    tb = {"n": 0, "rd": [None, None]}

    def tb_next():
        i = tb["n"] % 2
        tb["n"] += 1
        return PB[6 + i], (6 + i), tb["rd"][i]

    def tb_reader(bank_idx, ev):
        tb["rd"][bank_idx - 6] = ev

    slab_ev = [None] * SLOTS_PER_CHUNK

    xload_ev = [None] * (CHUNKS + 2)
    xt_busy = [None, None]
    xrow_ev = [None] * (CHUNKS + 2)
    xrow_cast_ev = [None] * (CHUNKS + 2)
    xfm_ready = [None] * (CHUNKS + 2)
    xtm_free = [None] * (CHUNKS + 2)
    xfm_free = [None] * (CHUNKS + 2)

    def emit_x_load(i):
        b, ci = divmod(i, SEQCH)
        c0 = ci * C
        E = Q if i < 2 else G
        if i >= 2:
            E.wait(xtm_free[i - 2])
            E.wait(xrow_cast_ev[i - 2])
        _xe = []
        for tt in range(NT):
            _xe.append(E.emit(lambda g, b=b, c0=c0, tt=tt, i=i: g.dma_start(
                x_tm[:, i % 2, tt, :], x_d[b, c0 + tt * 128:c0 + (tt + 1) * 128, :]),
                signal=True))
        xload_ev[i] = _combine(*_xe)
        if ci != SEQCH - 1:
            xrow_ev[i] = E.emit(lambda g, b=b, c0=c0, i=i: g.dma_start(
                xrow_f[:, i % 2, :], x_d[b, c0 + C:c0 + C + 1, :]), signal=True)

    def emit_x_ingest(i):
        b, ci = divmod(i, SEQCH)
        seq_start = ci == 0
        seq_end = ci == SEQCH - 1
        pi = i % 2
        if i >= 2:
            V.wait(xfm_free[i - 2])
        if seq_start:
            V.emit(lambda v, pi=pi: v.tensor_copy(x_fm[:, pi, :, 0:1], zbf.unsqueeze(2)))
        else:
            V.emit(lambda v, pi=pi: v.tensor_copy(
                x_fm[:, pi, :, 0:1], x_fm[:, 1 - pi, :, 256:257]))
        S.wait(xload_ev[i])
        if xt_busy[0] is not None:
            S.wait(xt_busy[0])
        e_xtc = S.emit(lambda s, pi=pi: s.copy(xt_bf, x_tm[:, pi, :, :]),
                       signal=True)
        for tt in range(NT):
            for g4 in range(2):
                ps, bidx, last_rd = tb_next()
                T.wait(e_xtc)
                if i >= 2:
                    T.wait(xfm_free[i - 2])
                if last_rd is not None:
                    T.wait(last_rd)
                if i == 0:
                    T.wait(e_eyebf)
                for kk in range(4):
                    k = g4 * 4 + kk
                    e_tr = T.emit(lambda t, tt=tt, k=k, kk=kk, ps=ps: t.transpose(
                        ps.bitcast(BF16)[:, kk * 128:(kk + 1) * 128],
                        xt_bf[:, tt, k * 128:(k + 1) * 128], eye_bf),
                        signal=(kk == 3))
                V.wait(e_tr)
                e_ev = V.emit(lambda v, ps=ps, pi=pi, g4=g4, tt=tt: v.tensor_copy(
                    x_fm[:, pi, g4 * 4:(g4 + 1) * 4, 1 + tt * 128:1 + (tt + 1) * 128],
                    ps.bitcast(BF16).rearrange("p (a b) -> p a b", b=128)[:, 0:4, :]),
                    signal=True)
                tb_reader(bidx, e_ev)
        xt_busy[0] = e_tr
        if seq_end:
            ev_h = V.emit(lambda v, pi=pi: v.tensor_copy(
                x_fm[:, pi, :, W - 1:W], zbf.unsqueeze(2)), signal=True)
        else:
            ps, bidx, last_rd = tb_next()
            T.wait(xrow_ev[i])
            if last_rd is not None:
                T.wait(last_rd)
            for k in range(ND):
                e_tr = T.emit(lambda t, k=k, ps=ps, pi=pi: t.transpose(
                    ps[:, k:k + 1], xrow_f[0:1, pi, k * 128:(k + 1) * 128],
                    eye_f[0:1, 0:1]), signal=(k == ND - 1))
            xrow_cast_ev[i] = e_tr
            V.wait(e_tr)
            ev_h = V.emit(lambda v, ps=ps, pi=pi: v.tensor_copy(
                x_fm[:, pi, :, W - 1:W],
                ps.rearrange("p (a b) -> p a b", b=1)[:, 0:ND, :]), signal=True)
            tb_reader(bidx, ev_h)
        xfm_ready[i] = ev_h

    emit_x_load(0)
    emit_x_ingest(0)
    if n_chunks > 1:
        emit_x_load(1)
        emit_x_ingest(1)

    pre = {}
    win_load_ev = []
    for k in range(ND):
        ev = G.emit(lambda g, k=k: g.dma_start(
            krow_aps[k], win_d[k * 128:(k + 1) * 128, :]), signal=True)
        win_load_ev.append(ev)

    def emit_A_staging():
        last_cp = None
        for s_old in range(16):
            s = s_old if s_old < 8 else 40 + (s_old - 8)
            stg, sb = stg_next()
            if stg_state["busy"][sb] is not None:
                S.wait(stg_state["busy"][sb])
            for jj in range(2):
                j = 2 * s_old + jj
                for k in range(ND):
                    src = krow_aps[k][:, j * 128:(j + 1) * 128]
                    S.wait(win_load_ev[k])
                    tp = jj * 1024 + k * 128
                    last_cp = S.emit(lambda s_, src=src, tp=tp, stg=stg: s_.copy(
                        stg[:, tp:tp + 128], src),
                        signal=(jj == 1 and k == ND - 1))
            Q.wait(sfence())
            ev_s = Q.emit(lambda q, s=s, stg=stg: q.dma_start(ws_d[s], stg),
                          signal=True)
            slab_ev[s] = ev_s
            stg_stored(sb, ev_s)
        pre["a_last_cp"] = last_cp

    ring_load_ev = []
    slot_done_ev = []
    next_load = [0]
    want_load = [RING_SLOTS]
    wo_store_ev = {}
    wo_r3 = wo_d.rearrange("(k p) n -> p k n", p=128)

    def ring_tile(q, t):
        rs = q % RING_SLOTS
        return ring[:, rs * SLOT_T + t, :]

    WM = list(range(0, 8))
    S1 = list(range(8, 24))
    S2 = list(range(24, 40))
    WR = list(range(40, 48))
    WO = list(range(48, 56))
    slot_slabs = list(WO)
    for i in range(n_chunks):
        if i + 1 < n_chunks:
            slot_slabs += WM + WR
        slot_slabs += S1 + S2
        if i + 1 < n_chunks:
            slot_slabs += WO

    def pump_ring():
        while next_load[0] < min(want_load[0], len(slot_slabs)):
            q = next_load[0]
            s = slot_slabs[q]
            if q >= 8 and slab_ev[s] is None:
                return
            if q >= RING_SLOTS:
                Q.wait(slot_done_ev[q - RING_SLOTS])
                if q - RING_SLOTS in wo_store_ev:
                    Q.wait(wo_store_ev[q - RING_SLOTS])
            if q in (8, 9):
                Q.wait(pre["a_last_cp"])
                Q.wait(xfm_free[0])
            rs = q % RING_SLOTS
            if q < 8:
                dj = s - 48
                dst = ring[:, rs * SLOT_T:(rs + 1) * SLOT_T, :]
                srcp = wo_r3[:, :, dj * 128:(dj + 1) * 128]
                ev = G.emit(lambda g, dst=dst, srcp=srcp: g.dma_start(dst, srcp),
                            signal=True)
            else:
                Q.wait(slab_ev[s])
                dst = ring[:, rs * SLOT_T:(rs + 1) * SLOT_T, :].rearrange(
                    "p a b -> p (a b)")
                ev = Q.emit(lambda g, s=s, dst=dst: g.dma_start(dst, ws_d[s]),
                            signal=True)
            ring_load_ev.append(ev)
            next_load[0] += 1

    def emit_wo_stores():
        for q in range(8):
            s = 48 + q
            rs = q % RING_SLOTS
            srcp = ring[:, rs * SLOT_T:(rs + 1) * SLOT_T, :].rearrange(
                "p a b -> p (a b)")
            Q.wait(ring_load_ev[q])
            ev = Q.emit(lambda g, s=s, srcp=srcp: g.dma_start(ws_d[s], srcp),
                        signal=True)
            slab_ev[s] = ev
            wo_store_ev[q] = ev

    def emit_preproc_C(lo, hi):
        c_sgn = pre.setdefault("c_sgn", [None, None])
        c_abs = pre.setdefault("c_abs", [None, None])
        c_tr = pre.setdefault("c_tr", [None, None])
        for hj in range(lo, hi):
            hb = hj % 2
            wr = wrow_f[:, hb * DM:(hb + 1) * DM]
            sro = srow_bf[:, hb * DM:(hb + 1) * DM]
            if c_sgn[hb] is not None:
                G.wait(c_sgn[hb])
                G.wait(c_abs[hb])
            if hj == 0:
                G.wait(pre["a_last_cp"])
                G.wait(xfm_free[0])
            e_l = G.emit(lambda q, hj=hj, wr=wr: q.dma_start(
                wr, w1_d[hj * 128:(hj + 1) * 128, :]), signal=True)
            S.wait(e_l)
            if c_tr[hb] is not None:
                S.wait(c_tr[hb])
            e_sgn = S.emit(lambda s, sro=sro, wr=wr: s.sign(sro, wr), signal=True)
            V.wait(e_l)
            e_abs = V.emitf(lambda v, hj=hj, wr=wr: v.reduce_sum(
                al1[:, hj:hj + 1], wr, AX.X, apply_absolute_value=True))
            ps, bidx, last_rd = tb_next()
            psv = ps.bitcast(BF16)
            T.wait(e_sgn)
            if hj == 0:
                T.wait(e_eyebf)
            if last_rd is not None:
                T.wait(last_rd)
            for k in range(ND):
                e_tr = T.emit(lambda t, k=k, psv=psv, sro=sro: t.transpose(
                    psv[:, k * 128:(k + 1) * 128], sro[:, k * 128:(k + 1) * 128],
                    eye_bf), signal=(k == ND - 1))
            S.wait(e_tr)
            if hj % 2 == 0:
                cur = stg_next()
                stg_state["cur"] = cur
                if stg_state["busy"][cur[1]] is not None:
                    S.wait(stg_state["busy"][cur[1]])
            stg, sb = stg_state["cur"]
            e_ev = S.emit(lambda s_, psv=psv, off=(hj % 2) * 1024, stg=stg: s_.copy(
                stg[:, off:off + 1024], psv[:, 0:1024]), signal=True)
            tb_reader(bidx, e_ev)
            if hj % 2 == 1:
                s = 8 + hj // 2
                Q.wait(sfence())
                slab_ev[s] = Q.emit(lambda q, s=s, stg=stg: q.dma_start(ws_d[s], stg),
                                    signal=True)
                stg_stored(sb, slab_ev[s])
            c_sgn[hb] = e_sgn
            c_abs[hb] = e_abs
            c_tr[hb] = e_tr
        if hi == NH:
            pre["sgn"] = c_sgn[1]
            pre["abs"] = c_abs[1]
            pre["tr"] = c_tr[1]
            V.emitf(lambda v: v.tensor_scalar_mul(al1, al1, 1.0 / DM))
        pump_ring()

    def emit_preproc_D(lo, hi):
        prev_sgn, prev_abs, prev_tr = pre["sgn"], pre["abs"], pre["tr"]
        for dj in range(lo, hi):
            if prev_sgn is not None:
                G.wait(prev_sgn)
            if prev_abs is not None:
                G.wait(prev_abs)
            e_l = G.emit(lambda q, dj=dj: q.dma_start(
                wrow_f, w2_d[dj * 128:(dj + 1) * 128, :]), signal=True)
            S.wait(e_l)
            if prev_tr is not None:
                S.wait(prev_tr)
            e_sgn = S.emit(lambda s: s.sign(srow_bf, wrow_f), signal=True)
            V.wait(e_l)
            e_abs = V.emitf(lambda v, dj=dj: v.reduce_sum(
                al2[:, dj:dj + 1], wrow_f, AX.X, apply_absolute_value=True))
            for g4 in range(4):
                ps, bidx, last_rd = tb_next()
                psv = ps.bitcast(BF16)
                T.wait(e_sgn)
                if last_rd is not None:
                    T.wait(last_rd)
                for kk in range(8):
                    k = g4 * 8 + kk
                    e_tr = T.emit(lambda t, k=k, kk=kk, psv=psv: t.transpose(
                        psv[:, kk * 128:(kk + 1) * 128],
                        srow_bf[:, k * 128:(k + 1) * 128],
                        eye_bf), signal=(kk == 7))
                S.wait(e_tr)
                s = 24 + dj * 2 + g4 // 2
                if g4 % 2 == 0:
                    cur = stg_next()
                    stg_state["cur"] = cur
                    if stg_state["busy"][cur[1]] is not None:
                        S.wait(stg_state["busy"][cur[1]])
                stg, sb = stg_state["cur"]
                e_ev = S.emit(lambda s_, psv=psv, off=(g4 % 2) * 1024, stg=stg:
                              s_.copy(stg[:, off:off + 1024], psv[:, 0:1024]),
                              signal=True)
                tb_reader(bidx, e_ev)
                if g4 % 2 == 1:
                    Q.wait(sfence())
                    slab_ev[s] = Q.emit(lambda q, s=s, stg=stg: q.dma_start(
                        ws_d[s], stg), signal=True)
                    stg_stored(sb, slab_ev[s])
            prev_sgn = e_sgn
            prev_abs = e_abs
            prev_tr = e_tr
        pre["sgn"], pre["abs"], pre["tr"] = prev_sgn, prev_abs, prev_tr
        if hi == ND:
            V.emitf(lambda v: v.tensor_scalar_mul(al2, al2, 1.0 / DH))
        pump_ring()

    tile_ctr = [0]

    def mm_group(ps_view, rhs_of_k, n_k, extra_wait=None, wait_of_k=None):
        last_ev = None
        for k in range(n_k):
            g = tile_ctr[0]
            tile_ctr[0] += 1
            q, t = divmod(g, SLOT_T)
            ap = ring_tile(q, t)
            T.wait(ring_load_ev[q])
            if extra_wait is not None and k == 0:
                T.wait(extra_wait)
            if wait_of_k is not None:
                T.wait(wait_of_k(k))
            is_slot_end = (t == SLOT_T - 1)
            sig = is_slot_end or (k == n_k - 1)
            rhs = rhs_of_k(k)
            ev = T.emit(lambda t_, ps=ps_view, ap=ap, rhs=rhs,
                        st=(k == 0), sp=(k == n_k - 1):
                        t_.matmul(ps, ap, rhs, start=st, stop=sp), signal=sig)
            if is_slot_end:
                slot_done_ev.append(ev)
                want_load[0] += 1
                pump_ring()
            last_ev = ev
        return last_ev

    def mm_group_direct(ps_view, rhs_of_k, n_k, j, res, extra_wait=None):
        last_ev = None
        for k in range(n_k):
            ap = krow_aps[k][:, res * 2048 + j * 128:res * 2048 + (j + 1) * 128]
            T.wait(win_load_ev[k])
            if extra_wait is not None and k == 0:
                T.wait(extra_wait)
            rhs = rhs_of_k(k)
            ev = T.emit(lambda t_, ps=ps_view, ap=ap, rhs=rhs,
                        st=(k == 0), sp=(k == n_k - 1):
                        t_.matmul(ps, ap, rhs, start=st, stop=sp),
                        signal=(k == n_k - 1))
            last_ev = ev
        return last_ev

    ln = {"res_done": None, "last_apply": None, "snap": None,
          "apply_tt": [None] * NT}

    def ln_front(src_fm, res_tm, e_src):
        e_sq = None
        for tt in range(NT):
            for half in range(2):
                ps, bidx, last_rd = tb_next()
                T.wait(e_src(half) if callable(e_src) else e_src)
                if last_rd is not None:
                    T.wait(last_rd)
                for kk in range(4):
                    dj = half * 4 + kk
                    e_tr = T.emit(lambda t, tt=tt, dj=dj, kk=kk, ps=ps: t.transpose(
                        ps.bitcast(F32R)[:, kk * 128:(kk + 1) * 128],
                        src_fm[:, dj, tt * 128:(tt + 1) * 128], eye_r), signal=(kk == 3))
                V.wait(e_tr)
                e_r = V.emit(lambda v, ps=ps, tt=tt, half=half: v.scalar_tensor_tensor(
                    r_tm[:, tt, half * 512:(half + 1) * 512], ps, 1.0,
                    res_tm[:, tt, half * 512:(half + 1) * 512], ALU.mult, ALU.add,
                    accum_out=st_sum[:, half, tt:tt + 1]), signal=True)
                tb_reader(bidx, e_r)
                S.wait(e_r)
                e_sq = S.emit(lambda s, tt=tt, half=half: s.activation(
                    junk, r_tm[:, tt, half * 512:(half + 1) * 512], AF.Square,
                    accum_out=st_sq[:, half, tt:tt + 1]), signal=True)
        ln["res_done"] = e_r
        return e_r, e_sq

    def ln_tail(dst_tm, e_r, e_sq):
        GC.wait(e_r)
        GC.wait(e_sq)
        if ln["last_apply"] is not None:
            GC.wait(ln["last_apply"])
        GC.emit(lambda g: g.tensor_add(st_m, st_sum[:, 0, :], st_sum[:, 1, :]))
        GC.emit(lambda g: g.tensor_scalar_mul(st_m, st_m, 1.0 / DM))
        GC.emit(lambda g: g.tensor_add(st_var, st_sq[:, 0, :], st_sq[:, 1, :]))
        GC.emit(lambda g: g.tensor_scalar_mul(st_var, st_var, 1.0 / DM))
        GC.emit(lambda g: g.tensor_mul(st_t, st_m, st_m))
        GC.emit(lambda g: g.tensor_sub(st_var, st_var, st_t))
        e_u = GC.emit(lambda g: g.tensor_scalar_add(st_var, st_var, EPS), signal=True)
        V.wait(e_u)
        V.emitf(lambda v: v.tensor_scalar(
            st_inv.bitcast(mybir.dt.int32), st_var.bitcast(mybir.dt.int32),
            1, None, ALU.arith_shift_right))
        e_seed = V.emitf(lambda v: v.tensor_tensor(
            st_inv.bitcast(mybir.dt.int32), magic_sb[:, 0:NT],
            st_inv.bitcast(mybir.dt.int32), ALU.subtract))
        GC.wait(e_seed)
        for _ in range(3):
            GC.emit(lambda g: g.tensor_mul(st_t2, st_inv, st_inv))
            GC.emit(lambda g: g.tensor_mul(st_t2, st_t2, st_var))
            GC.emit(lambda g: g.tensor_scalar(st_t2, st_t2, -0.5, 1.5,
                                              ALU.mult, ALU.add))
            GC.emit(lambda g: g.tensor_mul(st_inv, st_inv, st_t2))
        GC.emit(lambda g: g.tensor_scalar(
            st_nmi, st_m, -1.0, None, ALU.mult))
        e_gst = GC.emit(lambda g: g.tensor_mul(st_nmi, st_nmi, st_inv), signal=True)
        V.wait(e_gst)
        e_dst = None
        for tt in range(NT):
            e_dst = V.emit(lambda v, tt=tt: v.tensor_scalar(
                dst_tm[:, tt, :], r_tm[:, tt, :],
                st_inv[:, tt:tt + 1], st_nmi[:, tt:tt + 1],
                ALU.mult, ALU.add), signal=True)
            ln["apply_tt"][tt] = e_dst
        ln["last_apply"] = e_dst
        return e_dst

    def ln_block(src_fm, res_tm, dst_tm, e_src):
        e_r, e_sq = ln_front(src_fm, res_tm, e_src)
        return ln_tail(dst_tm, e_r, e_sq)

    mb_reader = [None] * 4
    state = {"bc_reader": None, "ys_reader": None}
    out_store_ev = [None] * CHUNKS
    last_sil = [None] * (CHUNKS + 1)
    sil_evs = [[None] * NCH for _ in range(CHUNKS + 1)]
    y_evs = [[None] * NCH for _ in range(CHUNKS + 1)]
    at_evs = [[None] * ND for _ in range(CHUNKS + 1)]
    h_evs = [[None] * NH for _ in range(CHUNKS + 1)]
    ff_evs = [[None] * ND for _ in range(CHUNKS + 1)]
    e_scv_l = [None] * (CHUNKS + 1)
    e_y_l = [None] * (CHUNKS + 1)
    e_x1_l = [None] * (CHUNKS + 1)
    x1_apply = [None] * (CHUNKS + 1)
    e_at_l = [None] * (CHUNKS + 1)
    e_h_l = [None] * (CHUNKS + 1)
    e_ff_l = [None] * (CHUNKS + 1)
    e_ys_l = [None] * (CHUNKS + 1)

    def ph_M(i, direct=False):
        pi = i % 2
        pend = [None, None]

        def emit_silu(pj, e_c3):
            S.wait(e_c3)
            ev = S.emit(lambda s, pj=pj, bank=pj % 4: s.activation(
                xc[:, pj, :], cv2[:, bank, :], AF.Silu, bias=cb_sb[:, pj:pj + 1]),
                signal=True)
            sil_evs[i][pj] = ev
            last_sil[i] = ev

        for j in range(NCH):
            bank = j % 4
            ps = PB[bank]
            if mb_reader[bank] is not None:
                T.wait(mb_reader[bank])
            if direct:
                e_mm = mm_group_direct(
                    ps[:, 0:W], lambda k, pi=pi: x_fm[:, pi, k, :], ND, j=j,
                    res=0, extra_wait=xfm_ready[i] if j == 0 else None)
            else:
                e_mm = mm_group(ps[:, 0:W], lambda k, pi=pi: x_fm[:, pi, k, :], ND,
                                extra_wait=xfm_ready[i] if j == 0 else None)
            S.wait(e_mm)
            e_c1 = S.emit(lambda s, ps=ps, j=j, bank=bank: s.activation(
                cv1[:, bank, :], ps[:, 0:C], AF.Identity, scale=wtap[:, j, 0:1]),
                signal=True)
            V.wait(e_c1)
            V.emit(lambda v, ps=ps, j=j, bank=bank: v.scalar_tensor_tensor(
                cv2[:, bank, :], ps[:, 1:C + 1], wtap[:, j, 1:2], cv1[:, bank, :],
                ALU.mult, ALU.add))
            e_c3 = V.emit(lambda v, ps=ps, j=j, bank=bank: v.scalar_tensor_tensor(
                cv2[:, bank, :], ps[:, 2:C + 2], wtap[:, j, 2:3], cv2[:, bank, :],
                ALU.mult, ALU.add), signal=True)
            mb_reader[bank] = e_c3
            if pend[0] is not None:
                emit_silu(*pend[0])
            pend[0] = pend[1]
            pend[1] = (j, e_c3)
        for p in pend:
            if p is not None:
                emit_silu(*p)

    def ph_BCscan(i):
        b, ci = divmod(i, SEQCH)
        seq_start = ci == 0
        if state["bc_reader"] is not None:
            T.wait(state["bc_reader"])
        e_bc = None
        for j in range(NCH):
            T.wait(sil_evs[i][j])
            T.emit(lambda t, j=j, st=(j == 0): t.matmul(
                bc_ps[0:16, 0:C], wb_r[:, j, :], xc[:, j, :], start=st, stop=False))
            e_bc = T.emit(lambda t, j=j, sig=(j == NCH - 1): t.matmul(
                bc_ps[0:16, C:2 * C], wc_r[:, j, :], xc[:, j, :],
                start=False, stop=sig), signal=(j == NCH - 1))
        if seq_start:
            V.emit(lambda v: v.tensor_copy(carry, zeroC[0:16, 0:1]))
        V.wait(e_bc)
        V.wait(e_decbc)
        V.emitf(lambda v: v.tensor_tensor_scan(
            s_sc, dec_bc, bc_ps[0:16, 0:C], carry, ALU.mult, ALU.add))
        V.emitf(lambda v: v.tensor_copy(carry, s_sc[:, C - 1:C]))
        e_scv_l[i] = V.emit(lambda v: v.tensor_tensor(
            sc_r[0:16, :], s_sc, bc_ps[0:16, C:2 * C], ALU.mult), signal=True)
        state["bc_reader"] = e_scv_l[i]

    def ph_ysR(i, direct=False):
        pi = i % 2
        if state["ys_reader"] is not None:
            T.wait(state["ys_reader"])
        T.wait(e_scv_l[i])
        T.wait(e_onesp)
        e_ys = T.emit(lambda t: t.matmul(ys_ps, onesp, sc_r, start=True, stop=True),
                      signal=True)
        e_ys_l[i] = e_ys
        e_y = None
        for j in range(NCH):
            bank = j % 4
            ps = PB[bank]
            if mb_reader[bank] is not None:
                T.wait(mb_reader[bank])
            if direct:
                e_mm = mm_group_direct(
                    ps[:, 0:C], lambda k, pi=pi: x_fm[:, pi, k, 1:C + 1], ND,
                    j=j, res=1)
            else:
                e_mm = mm_group(ps[:, 0:C],
                                lambda k, pi=pi: x_fm[:, pi, k, 1:C + 1], ND)
            if j == NCH - 1:
                xfm_free[i] = e_mm
            S.wait(e_mm)
            e_sr = S.emit(lambda s, ps=ps, bank=bank: s.activation(
                sr[:, bank, :], ps[:, 0:C], AF.Silu), signal=True)
            V.wait(e_ys)
            V.wait(e_sr)
            V.emit(lambda v, j=j, bank=bank: v.scalar_tensor_tensor(
                t2[:, bank, :], xc_f[:, j, :], dv_sb[:, j:j + 1], ys_ps,
                ALU.mult, ALU.add))
            e_y = V.emit(lambda v, j=j, bank=bank: v.tensor_tensor(
                y_bf[:, j, :], t2[:, bank, :], sr[:, bank, :], ALU.mult), signal=True)
            mb_reader[bank] = e_y
            y_evs[i][j] = e_y
        S.emit(lambda s: s.activation(junk[0:1, 0:1], fence_buf[0:1, 0:1], AF.Gelu))
        e_y_l[i] = e_y
        state["ys_reader"] = e_y

    def ph_W(i):
        for dj in range(ND):
            bank = dj % 4
            ps = PB[bank]
            if mb_reader[bank] is not None:
                T.wait(mb_reader[bank])
            e_mm = mm_group(ps[:, 0:C], lambda k: y_bf[:, k, :], NCH,
                            wait_of_k=lambda k, i=i: y_evs[i][k])
            S.wait(e_mm)
            e_at = S.emit(lambda s, ps=ps, dj=dj: s.activation(
                attn_sb[:, dj, :], ps[:, 0:C], AF.Identity), signal=True)
            mb_reader[bank] = e_at
            at_evs[i][dj] = e_at

    def ph_T1_front(i):
        pi = i % 2
        ev = ln_front(attn_sb, x_tm[:, pi, :, :],
                      lambda half, i=i: at_evs[i][half * 4 + 3])
        xtm_free[i] = ln["res_done"]
        return ev

    def ph_T1_tail(i, ev):
        e_x1_l[i] = ln_tail(x1_tm, *ev)
        x1_apply[i] = list(ln["apply_tt"])

    def ph_T2(i):
        e_x1fm = None
        x1f_done = {}
        for tt in range(NT):
            for g4 in range(2):
                ps, bidx, last_rd = tb_next()
                T.wait(x1_apply[i][tt])
                if last_rd is not None:
                    T.wait(last_rd)
                for kk in range(4):
                    k = g4 * 4 + kk
                    e_tr = T.emit(lambda t, tt=tt, k=k, kk=kk, ps=ps: t.transpose(
                        ps.bitcast(F32R)[:, kk * 128:(kk + 1) * 128],
                        x1_tm[:, tt, k * 128:(k + 1) * 128], eye_r), signal=(kk == 3))
                V.wait(e_tr)
                e_x1fm = V.emit(lambda v, ps=ps, g4=g4, tt=tt: v.tensor_copy(
                    x1_fm[:, g4 * 4:(g4 + 1) * 4, tt * 128:(tt + 1) * 128],
                    ps.rearrange("p (a b) -> p a b", a=4)), signal=True)
                tb_reader(bidx, e_x1fm)
                if tt == NT - 1:
                    x1f_done[g4] = e_x1fm
        return x1f_done

    def ph_H(i, x1f_done):
        e_h = None
        for hj in range(NH):
            bank = hj % 4
            ps = PB[bank]
            if mb_reader[bank] is not None:
                T.wait(mb_reader[bank])
            e_mm = mm_group(ps[:, 0:C], lambda k: x1_fm[:, k, :], ND,
                            wait_of_k=lambda k: x1f_done[k // 4])
            S.wait(e_mm)
            e_h = S.emit(lambda s, ps=ps, hj=hj: s.activation(
                h_bf[:, hj, :], ps[:, 0:C], AF.Gelu,
                bias=b1_sb[:, hj:hj + 1], scale=al1[:, hj:hj + 1]), signal=True)
            mb_reader[bank] = e_h
            h_evs[i][hj] = e_h
        S.emit(lambda s: s.activation(junk[0:1, 0:1], fence_buf[0:1, 0:1], AF.Silu))
        e_h_l[i] = e_h

    def ph_O(i):
        for dj in range(ND):
            bank = dj % 4
            ps = PB[bank]
            if mb_reader[bank] is not None:
                T.wait(mb_reader[bank])
            e_mm = mm_group(ps[:, 0:C], lambda k: h_bf[:, k, :], NH,
                            wait_of_k=lambda k, i=i: h_evs[i][k])
            S.wait(e_mm)
            e_ff = S.emit(lambda s, ps=ps, dj=dj: s.activation(
                attn_sb[:, dj, :], ps[:, 0:C], AF.Identity,
                bias=b2_sb[:, dj:dj + 1], scale=al2[:, dj:dj + 1]), signal=True)
            mb_reader[bank] = e_ff
            ff_evs[i][dj] = e_ff

    def ph_T3(i):
        b, ci = divmod(i, SEQCH)
        c0 = ci * C
        pi = i % 2
        if i >= 2 and out_store_ev[i - 2] is not None:
            V.wait(out_store_ev[i - 2])
        e_out = ln_block(attn_sb, x1_f, out_tm[:, pi, :, :],
                         lambda half, i=i: ff_evs[i][half * 4 + 3])
        _oe = []
        for tt in range(NT):
            G.wait(ln["apply_tt"][tt])
            _oe.append(G.emit(lambda g, b=b, c0=c0, tt=tt, pi=pi: g.dma_start(
                out_d[b, c0 + tt * 128:c0 + (tt + 1) * 128, :], out_tm[:, pi, tt, :]),
                signal=True))
        out_store_ev[i] = _combine(*_oe)

    def emit_debug(plimit=99):
        if plimit >= 2:
            e_l = V.emit(lambda v: v.tensor_copy(junk[0:16, 0:2 * C], bc_ps[0:16, :]),
                         signal=True)
            G.wait(e_l)
            G.emit(lambda g: g.dma_start(dbg_outs["d_bc"], junk[0:16, 0:2 * C]),
                   signal=True)
            G.emit(lambda g: g.dma_start(dbg_outs["d_ssc"], s_sc), signal=True)
            G.emit(lambda g: g.dma_start(dbg_outs["d_scr"], sc_r), signal=True)
            G.emit(lambda g: g.dma_start(dbg_outs["d_dec"], dec_bc), signal=True)
            G.emit(lambda g: g.dma_start(dbg_outs["d_ones"], onesp), signal=True)
        if plimit >= 2:
            e_l = V.emit(lambda v: v.tensor_copy(junk[:, 0:C], ys_ps), signal=True)
            G.wait(e_l)
            G.emit(lambda g: g.dma_start(dbg_outs["d_ys"], junk[:, 0:C]), signal=True)
        pairs = [("d_al1", al1, 0), ("d_b1", b1_sb, 0), ("d_al2", al2, 0),
                 ("d_x1fm", x1_fm, 6),
                 ("d_xc", xc, 1), ("d_x1", x1_tm, 5), ("d_h", h_bf, 7),
                 ("d_y", y_bf, 3), ("d_ffn", attn_sb, 8), ("d_r", r_tm, 4),
                 ("d_stm", st_m, 4), ("d_stv", st_var, 4), ("d_sti", st_inv, 4),
                 ("d_ssum", st_sum, 4), ("d_ssq", st_sq, 4)]
        G.wait(_Ev("s_v", V.count))
        G.wait(_Ev("s_s", S.count))
        for nm, srcap, need in pairs:
            if plimit >= need:
                G.emit(lambda g, nm=nm, srcap=srcap: g.dma_start(dbg_outs[nm], srcap),
                       signal=True)

    ph_M(0, direct=True)
    ph_BCscan(0)
    emit_A_staging()
    ph_ysR(0, direct=True)
    pump_ring()
    emit_wo_stores()
    ph_W(0)
    for i in range(n_chunks):
        ev_t1 = ph_T1_front(i)
        if i == 0:
            emit_preproc_C(0, 8)
        if i + 1 < n_chunks:
            ph_M(i + 1)
        if i == 0:
            emit_preproc_C(8, 16)
        if i + 1 < n_chunks:
            ph_BCscan(i + 1)
        if i + 2 < n_chunks:
            emit_x_load(i + 2)
            emit_x_ingest(i + 2)
        ph_T1_tail(i, ev_t1)
        if i == 0:
            emit_preproc_C(16, 24)
        if i + 1 < n_chunks:
            ph_ysR(i + 1)
        if i == 0:
            emit_preproc_C(24, 32)
            emit_preproc_D(0, 4)
        x1f_done = ph_T2(i)
        if i == 0:
            emit_preproc_D(4, 8)
        ph_H(i, x1f_done)
        ph_O(i)
        ph_T3(i)
        if i + 1 < n_chunks:
            ph_W(i + 1)

    if debug == "ws":
        for j in range(Q.NPOOL):
            n_j = (Q.ndma - j + Q.NPOOL - 1) // Q.NPOOL
            if n_j:
                G.wait(_Ev(f"{Q.semname}{j}", 16 * n_j))
        for s in range(SLOTS_PER_CHUNK):
            G.emit(lambda g, s=s: g.dma_start(wsdump[s], ws_d[s]), signal=True)
    elif debug:
        emit_debug(DEBUG_PHASE_LIMIT[0])

    import contextlib
    with contextlib.ExitStack() as ctx:
        for eng in (T, V, S, Q, G, GC):
            for nm in eng.semnames():
                p.sems[nm] = ctx.enter_context(nc.semaphore(nm))
        with nc.Block() as blk:
            blk.tensor(lambda e: [op(e) for op in T.ops])
            blk.vector(lambda e: [op(e) for op in V.ops])
            blk.scalar(lambda e: [op(e) for op in S.ops])
            blk.sync(lambda e: [op(e) for op in Q.ops])
            blk.gpsimd(lambda e: [op(e) for op in G.ops])
    return nc


DEBUG_PHASE_LIMIT = [99]


def debug_phase_limit(i, n_chunks):
    return DEBUG_PHASE_LIMIT[0] if i == n_chunks - 1 else 99


_NC_CACHE = {}


def kernel(L0_attn_x, L0_attn_W_in, L0_attn_conv_w, L0_attn_conv_b, L0_attn_W_B,
           L0_attn_W_C, L0_attn_A, L0_attn_D, L0_attn_W_out, L0_ffn_w1, L0_ffn_b1,
           L0_ffn_w2, L0_ffn_b2):
    x = np.ascontiguousarray(np.asarray(L0_attn_x, dtype=np.float32))
    common = dict(
        W_in=np.ascontiguousarray(np.asarray(L0_attn_W_in, np.float32)),
        conv_w=np.ascontiguousarray(np.asarray(L0_attn_conv_w, np.float32).reshape(DI, 3)),
        conv_b=np.ascontiguousarray(np.asarray(L0_attn_conv_b, np.float32)),
        W_B=np.ascontiguousarray(np.asarray(L0_attn_W_B, np.float32)),
        W_C=np.ascontiguousarray(np.asarray(L0_attn_W_C, np.float32)),
        A=np.ascontiguousarray(np.asarray(L0_attn_A, np.float32)),
        Dv=np.ascontiguousarray(np.asarray(L0_attn_D, np.float32)),
        W_out=np.ascontiguousarray(np.asarray(L0_attn_W_out, np.float32)),
        w1=np.ascontiguousarray(np.asarray(L0_ffn_w1, np.float32)),
        b1=np.ascontiguousarray(np.asarray(L0_ffn_b1, np.float32)),
        w2=np.ascontiguousarray(np.asarray(L0_ffn_w2, np.float32)),
        b2=np.ascontiguousarray(np.asarray(L0_ffn_b2, np.float32)),
        eye=np.eye(128, dtype=np.float32),
        czeros=np.zeros((128, 256), dtype=np.float32),
        cones=np.ones((128, 256), dtype=np.float32),
        cmagic=np.full((128, 8), 0x5F3759DF, dtype=np.int32),
    )
    if "nc" not in _NC_CACHE:
        _NC_CACHE["nc"] = build_program()
    nc = _NC_CACHE["nc"]
    in_maps = []
    for c in range(NCORE):
        m = dict(common)
        m["x"] = np.ascontiguousarray(x[c * BPC:(c + 1) * BPC])
        in_maps.append(m)
    res = run_bass_kernel_spmd(nc, in_maps, core_ids=list(range(NCORE)))
    return np.concatenate([res.results[c]["out"] for c in range(NCORE)], axis=0)
